# version 02 (final)
# baseline (speedup 1.0000x reference)
import contextlib
"""Raw-Bass megakernel for mk_hybrid_d4 (8 NeuronCores, batch-sharded).

Per-core layout (batch b = core id):
  - residual stream: internal DRAM "resid" [1024, 4096] fp32, feature-major
    (row d, col t).
  - "ln_dram" [1024, 4096] bf16: LN(resid) for the current sublayer.
  - SBUF feature-major tiles [128 parts (d%128... d-chunk d//128), chunk, t].
  - matmuls: bf16 operands, fp32 PSUM accumulation; elementwise fp32.

Emission: per-engine step queues with explicit semaphore tokens.
Each semaphore is incremented by exactly one engine / DMA queue, so
absolute wait targets are exact; same-queue DMAs complete in issue order.
"""
import math
import os
import numpy as np
import concourse.bass as bass
import concourse.mybir as mybir
from concourse.bass_utils import run_bass_kernel_spmd
from concourse.aot_env import allow_ici_all_to_all

F32 = mybir.dt.float32
BF16 = mybir.dt.bfloat16
I32 = mybir.dt.int32
AF = mybir.ActivationFunctionType
ALU = mybir.AluOpType

D = 1024
T = 4096
NB = 8
NDC = D // 128
LN_EPS = 1e-5

ENGS = ["tensor", "vector", "scalar", "gpsimd", "sync"]
SEMS = ["mm", "ev", "es", "qg", "qs", "qc", "cc", "bar"]


def ts(i, n):
    return bass.ts(i, n)


class Bld:
    SERIAL = False

    def __init__(self, nc, block, sems, consts, dram):
        self.nc = nc
        self.block = block
        self.h = sems
        self.c = {k: 0 for k in SEMS}
        self.consts = consts
        self.dram = dram

    def wait(self, eng, tok):
        if tok is None:
            return
        sem, v = tok
        if v > 0:
            eng.wait_ge(self.h[sem], v)

    def barrier(self):
        blk = self.block
        base = self.c["bar"]
        n = len(ENGS)
        self.c["bar"] += n
        h = self.h["bar"]
        for name in ENGS:
            deco = getattr(blk, name)

            @deco
            def _(eng, _h=h, _t=base + n):
                eng.nop().then_inc(_h, 1)
                eng.wait_ge(_h, _t)


class P:

    def __init__(self, b):
        self.b = b
        self.q = {e: [] for e in ENGS}

    def step(self, eng, waits, emit, inc=None, amt=1, fence=False,
             noinst=False):
        b = self.b
        wl = [w for w in waits if w is not None]
        if Bld.SERIAL:
            wl = [(s, b.c[s]) for s in ("mm", "ev", "es", "qg", "qs", "qc")]
        tok = None
        if inc is not None:
            b.c[inc] += amt
            tok = (inc, b.c[inc])

        def run(e, _emit=emit, _wl=wl, _inc=inc, _amt=amt, _f=fence,
                _ni=noinst):
            for w in _wl:
                b.wait(e, w)
            ins = _emit(e)
            if _f:
                ins = e.drain()
            if _inc is not None and not _ni:
                ins.then_inc(b.h[_inc], _amt)

        self.q[eng].append(run)
        return tok

    def flush(self):
        blk = self.b.block
        for name in ENGS:
            if not self.q[name]:
                continue
            deco = getattr(blk, name)

            @deco
            def _(eng, _steps=self.q[name]):
                for st in _steps:
                    st(eng)
        self.q = {e: [] for e in ENGS}


def host_consts():
    c = {}
    c["ident_f32"] = np.eye(128, dtype=np.float32)
    c["identb"] = np.eye(128, dtype=np.float32)
    c["ones_col"] = np.ones((128, 1), dtype=np.float32)
    c["ones_row"] = np.ones((1, 128), dtype=np.float32)
    c["ones_rowf"] = np.ones((1, 128), dtype=np.float32)
    sel = np.zeros((8, 8 * 128), dtype=np.float32)
    for r in range(8):
        sel[r, r * 128:(r + 1) * 128] = 1.0
    c["sel8"] = sel
    c["sel8b"] = sel
    m = np.zeros((128, 128), dtype=np.float32)
    for i in range(128):
        m[i, i:] = 1.0
    c["mask128"] = m
    c["mask128f"] = m
    e = np.zeros((16, 16 * 128), dtype=np.float32)
    for hh in range(16):
        e[hh, hh * 128:(hh + 1) * 128] = 1.0
    c["eexp16"] = e
    c["e16"] = np.eye(16, dtype=np.float32)
    ep = np.zeros((128, 2), dtype=np.float32)
    ep[:, 0] = 1e-5
    ep[:, 1] = 1e-6
    c["epscol"] = ep
    c["ones8col"] = np.ones((8, 64), dtype=np.float32)
    bc2 = np.zeros((8, 512), dtype=np.float32)
    for s4 in range(4):
        for p in range(128):
            bc2[s4 * 2 + p // 64, s4 * 128 + p] = 1.0
    c["bc2c"] = bc2
    return c


CONST_SPECS = {
    "ident_f32": ([128, 128], F32),
    "identb": ([128, 128], BF16),
    "ones_col": ([128, 1], BF16),
    "ones_row": ([1, 128], BF16),
    "ones_rowf": ([1, 128], F32),
    "sel8": ([8, 8 * 128], F32),
    "sel8b": ([8, 8 * 128], BF16),
    "mask128": ([128, 128], BF16),
    "mask128f": ([128, 128], F32),
    "eexp16": ([16, 16 * 128], BF16),
    "e16": ([16, 16], BF16),
    "epscol": ([128, 2], F32),
    "ones8col": ([8, 64], F32),
    "bc2c": ([8, 512], F32),
}


def col_of(vec):
    return np.ascontiguousarray(vec.reshape(NDC, 128).T)[:, :, None]


def emit_ln_pass(b, eps=LN_EPS, src="resid", dst="ln_dram", gb=None):
    nc = b.nc
    resid = b.dram[src].ap().rearrange("(c p) t -> p c t", p=128)
    lnd = b.dram[dst].ap().rearrange("(c p) t -> p c t", p=128)
    TC = 512
    NT = T // TC

    with (
        nc.sbuf_tensor("ln_gb", [128, NDC, 128], F32) as gbt,
        nc.sbuf_tensor("ln_xb", [128, 2, NDC, TC], BF16) as xb,
        nc.sbuf_tensor("ln_x2", [128, NDC, TC], BF16) as x2,
        nc.sbuf_tensor("ln_s1", [128, 8], F32) as s1,
        nc.sbuf_tensor("ln_mu", [128, 4], F32) as mu,
        nc.sbuf_tensor("ln_va", [128, 4], F32) as va,
        nc.sbuf_tensor("ln_st", [8, 128], F32) as stt,
        nc.sbuf_tensor("ln_bc", [128, 2, TC], BF16) as bc,
        nc.sbuf_tensor("ln_out", [128, 2, NDC, TC], BF16) as lno,
        nc.sbuf_tensor("ln_ones", [128, 1], BF16) as lones,
        nc.psum_tensor("ln_ps", [128, 4], F32) as ps,
        nc.psum_tensor("ln_ps2", [128, 4], F32) as ps2,
        nc.psum_tensor("ln_pt", [8, 128], F32) as pt,
        nc.psum_tensor("ln_pb", [128, 2, TC], F32) as pb,
    ):
        p = P(b)
        gbtok = None
        if gb is not None:
            gbtok = p.step("gpsimd", [],
                           lambda g: g.dma_start(out=gbt[:, :, :],
                                                 in_=b.dram[gb].ap()),
                           inc="qg", amt=16)
        import os as _os
        if _os.environ.get("LOCALONES"):
            zo = p.step("vector", [], lambda v: v.memset(lones[:, :], 1.0),
                        inc="ev")
        ld = [None] * NT
        x2t = [None] * NT
        smm = [None] * NT
        tm1 = [None] * NT
        es1 = [None] * NT
        tm2 = [None] * NT
        trp = [None] * NT
        sev = [None] * NT
        bmm = [None] * NT
        bev = [None] * NT
        app = [None] * NT
        wrt = [None] * NT

        for i in range(NT):
            bi = i % 2
            ld[i] = p.step(
                "gpsimd", [app[i - 2] if i >= 2 else None],
                lambda g, _i=i, _bi=bi: g.dma_start(
                    out=xb[:, _bi, :, :], in_=resid[:, :, ts(_i, TC)]),
                inc="qg", amt=16)
            x2t[i] = p.step(
                "scalar", [ld[i], smm[i - 1] if i >= 1 else None],
                lambda s, _bi=bi: s.activation(
                    x2[:, :, :], xb[:, _bi, :, :], AF.Square),
                inc="es")

            def stats(t, _bi=bi):
                import os as _os
                ones = lones if _os.environ.get("LOCALONES") else b.consts["ones_col"]
                ins = None
                for tt in range(4):
                    for dc in range(NDC):
                        ins = t.matmul(ps[:, tt:tt + 1],
                                       xb[:, _bi, dc, ts(tt, 128)], ones[:, :],
                                       start=(dc == 0), stop=(dc == NDC - 1))
                for tt in range(4):
                    for dc in range(NDC):
                        ins = t.matmul(ps2[:, tt:tt + 1],
                                       x2[:, dc, ts(tt, 128)], ones[:, :],
                                       start=(dc == 0), stop=(dc == NDC - 1))
                return ins

            smm[i] = p.step("tensor", [x2t[i], tm1[i - 1] if i >= 1 else None]
                            + ([zo] if _os.environ.get("LOCALONES") and i == 0
                               else []),
                            stats, inc="mm")

            def tmmath(v):
                v.tensor_single_scalar(mu[:, :], ps[:, 0:4], 1.0 / D, ALU.mult)
                v.tensor_single_scalar(va[:, :], ps2[:, 0:4], 1.0 / D, ALU.mult)
                v.drain()
                v.tensor_mul(s1[:, 4:8], mu[:, :], mu[:, :])
                v.drain()
                return v.tensor_sub(va[:, :], va[:, :], s1[:, 4:8])

            if "ln_sdbg" in b.dram and i == 0:
                sda = b.dram["ln_sdbg"].ap()
                dd = p.step("vector", [smm[i]],
                            lambda v: (v.tensor_copy(mu[:, :], ps[:, 0:4]),
                                       v.tensor_copy(va[:, :], ps2[:, 0:4]),
                                       v.drain())[-1], inc="ev")
                p.step("sync", [dd],
                       lambda sy: sy.dma_start(out=sda[:, 0:4], in_=mu[:, :]),
                       inc="qs", amt=16)
                p.step("sync", [dd],
                       lambda sy: sy.dma_start(out=sda[:, 4:8], in_=va[:, :]),
                       inc="qs", amt=16)
            tm1[i] = p.step("vector", [smm[i], trp[i - 1] if i >= 1 else None],
                            tmmath, inc="ev")

            def rstd(s):
                epc = 0 if abs(eps - 1e-5) < 1e-9 else 1
                s.activation(va[:, :], va[:, :], AF.Ln,
                             bias=b.consts["epscol"][:, epc:epc + 1])
                s.drain()
                return s.activation(s1[:, 0:4], va[:, :], AF.Exp, scale=-0.5)

            es1[i] = p.step("scalar", [tm1[i]], rstd, inc="es")
            tm2[i] = p.step("vector", [es1[i]],
                            lambda v: v.tensor_mul(s1[:, 4:8], mu[:, :],
                                                   s1[:, 0:4]), inc="ev")
            trp[i] = p.step("tensor", [tm2[i], sev[i - 1] if i >= 1 else None],
                            lambda t: t.transpose(
                                pt[:, :], s1[:, :],
                                b.consts["ident_f32"][:, :]), inc="mm")
            sev[i] = p.step("vector", [trp[i], bmm[i - 1] if i >= 1 else None],
                            lambda v: v.tensor_copy(stt[:, :], pt[:, :]),
                            inc="ev")

            def bmmf(t):
                sel = b.consts["sel8"]
                ins = None
                for tt in range(4):
                    t.matmul(pb[:, 0, ts(tt, 128)], sel[:, ts(tt, 128)],
                             stt[:, :], start=True, stop=True)
                    ins = t.matmul(pb[:, 1, ts(tt, 128)],
                                   sel[:, ts(4 + tt, 128)],
                                   stt[:, :], start=True, stop=True)
                return ins

            bmm[i] = p.step("tensor", [sev[i], bev[i - 1] if i >= 1 else None],
                            bmmf, inc="mm")

            def bevf(v):
                v.tensor_copy(bc[:, 0, :], pb[:, 0, :])
                v.tensor_copy(bc[:, 1, :], pb[:, 1, :])
                return v.drain()

            bev[i] = p.step("vector", [bmm[i]], bevf, inc="ev")

            def appf(v, _bi=bi):
                ins = None
                for dc in range(NDC):
                    v.tensor_mul(lno[:, _bi, dc, :], xb[:, _bi, dc, :],
                                 bc[:, 0, :])
                for dc in range(NDC):
                    ins = v.tensor_sub(lno[:, _bi, dc, :],
                                       lno[:, _bi, dc, :], bc[:, 1, :])
                if gb is not None:
                    for dc in range(NDC):
                        ins = v.tensor_scalar(lno[:, _bi, dc, :],
                                              lno[:, _bi, dc, :],
                                              gbt[:, dc, 0:1], gbt[:, dc, 1:2],
                                              ALU.mult, ALU.add)
                return ins

            if "ln_dbg" in b.dram:
                dbga = b.dram["ln_dbg"].ap()
                p.step("sync", [bev[i]],
                       lambda sy, _i=i: sy.dma_start(
                           out=dbga[_i * 3:_i * 3 + 1, :],
                           in_=stt.rearrange("p x -> p x")[0:1, :]
                           if False else stt[0:1, :]),
                       inc="qs", amt=16)
                p.step("sync", [bev[i]],
                       lambda sy, _i=i: sy.dma_start(
                           out=dbga[_i * 3 + 1:_i * 3 + 2, 0:8],
                           in_=s1[0:1, :]),
                       inc="qs", amt=16)
            app[i] = p.step("vector", [bev[i], wrt[i - 2] if i >= 2 else None,
                                       gbtok if i == 0 else None],
                            appf, inc="ev")
            wrt[i] = p.step("sync", [app[i]],
                            lambda sy, _i=i, _bi=bi: sy.dma_start(
                                out=lnd[:, :, ts(_i, TC)],
                                in_=lno[:, _bi, :, :]),
                            inc="qs", amt=16)

        p.step("sync", [wrt[NT - 1]], lambda sy: sy.nop())
        p.flush()
    b.barrier()


def emit_rwkv(b, pre):
    nc = b.nc
    resid = b.dram["resid"].ap().rearrange("(c p) t -> p c t", p=128)
    lnd = b.dram["ln_dram"].ap().rearrange("(c p) t -> p c t", p=128)
    TC = 256
    NT = T // TC
    CU, CW, CMK, CMV, CMR = range(5)

    with (
        nc.sbuf_tensor("wk_w", [128, 4, NDC, D], BF16) as W,
        nc.sbuf_tensor("wk_c5", [128, NDC, 128], F32) as c5f,
        nc.sbuf_tensor("wk_cx", [128, NDC, 4], F32) as cx,
        
        nc.sbuf_tensor("wk_ln", [128, 2, NDC, TC + 1], BF16) as lnc,
        nc.sbuf_tensor("wk_mx", [128, NDC, TC], BF16) as mx,
        nc.sbuf_tensor("wk_kf", [128, NDC, TC], F32) as kf,
        nc.sbuf_tensor("wk_qf", [128, NDC, TC], F32) as qf,
        nc.sbuf_tensor("wk_vf", [128, NDC, TC], F32) as vf,
        nc.sbuf_tensor("wk_aa", [128, NDC, TC], F32) as aa,
        nc.sbuf_tensor("wk_bb", [128, NDC, TC], F32) as bbf,
        nc.sbuf_tensor("wk_rr", [128, NDC, TC], F32) as rr,
        nc.sbuf_tensor("wk_cab", [128, NDC, 2], F32) as cab,
        nc.sbuf_tensor("wk_wr", [128, NDC, TC], BF16) as wkvr,
        nc.sbuf_tensor("wk_xr", [128, NDC, TC], F32) as xr,
        nc.psum_tensor("wk_pp", [128, 2, 512], F32) as pp,
    ):
        c5 = c5f[:, :, 0:5]
        boc = c5f[:, :, 5:6]
        p = P(b)
        wtok = []
        for wi, nm in enumerate(["WkT", "WvT", "WrT", "WoT"]):
            ap = b.dram[pre + nm].ap().rearrange("(c p) o -> p c o", p=128)
            wtok.append(p.step("gpsimd", [],
                               lambda g, _wi=wi, _ap=ap: g.dma_start(
                                   out=W[:, _wi, :, :], in_=_ap),
                               inc="qg", amt=16))
        ctok = p.step("gpsimd", [],
                      lambda g: g.dma_start(out=c5f[:, :, :],
                                            in_=b.dram[pre + "cols5"].ap()),
                      inc="qg", amt=16)
        botok = ctok

        def mkcx(v):
            v.tensor_single_scalar(cx[:, :, 0:1], c5[:, :, CW:CW + 1], -1.0,
                                   ALU.mult)
            for j, ci in enumerate([CMK, CMV, CMR]):
                v.tensor_scalar(cx[:, :, 1 + j:2 + j], c5[:, :, ci:ci + 1],
                                -1.0, 1.0, ALU.mult, ALU.add)
            v.memset(cab[:, :, :], 0.0)
            return v.drain()

        cxt = p.step("vector", [ctok], mkcx, inc="ev")
        z0 = p.step("vector", [],
                    lambda v: v.memset(lnc[:, 0, :, 0:1], 0.0), inc="ev")
        z1 = p.step("vector", [],
                    lambda v: v.memset(lnc[:, 1, :, 0:1], 0.0), inc="ev")

        mix_done = [None] * NT
        scan_done = [None] * NT
        st_all = [None] * NT
        ld = [None] * NT
        lead_tok = [None] * NT
        evo_fin = [None] * NT

        for i in range(NT):
            bi = i % 2

            ld[i] = p.step("sync",
                           [z0 if i == 0 else None, z1 if i == 1 else None,
                            mix_done[i - 2] if i >= 2 else None,
                            lead_tok[i - 1] if i >= 1 else None],
                           lambda sy, _i=i, _bi=bi: sy.dma_start(
                               out=lnc[:, _bi, :, 1:TC + 1],
                               in_=lnd[:, :, ts(_i, TC)]),
                           inc="qs", amt=16)
            if i > 0:
                lead_tok[i] = p.step("vector",
                       [ld[i - 1], mix_done[i - 2] if i >= 2 else None],
                       lambda v, _bi=bi: _leadcopy(v, lnc, _bi, TC),
                       inc="ev")

            def mk_mix(ci, cxi):
                def f(v, _bi=bi, _ci=ci, _cxi=cxi):
                    for dc in range(NDC):
                        v.tensor_scalar(mx[:, dc, :], lnc[:, _bi, dc, 0:TC],
                                        cx[:, dc, _cxi:_cxi + 1], None,
                                        ALU.mult)
                    ins = None
                    for dc in range(NDC):
                        ins = v.scalar_tensor_tensor(
                            mx[:, dc, :], lnc[:, _bi, dc, 1:TC + 1],
                            c5[:, dc, _ci:_ci + 1], mx[:, dc, :],
                            ALU.mult, ALU.add)
                    return ins
                return f

            def proj(wi, oc):
                def f(t, _wi=wi, _oc=oc):
                    ins = None
                    for ic in range(NDC):
                        ins = t.matmul(pp[:, _oc % 2, 0:TC],
                                       W[:, _wi, ic, ts(_oc, 128)],
                                       mx[:, ic, :], start=(ic == 0),
                                       stop=(ic == NDC - 1))
                    return ins
                return f

            mxk = p.step("vector", [ld[i], cxt,
                                    mix_done[i - 1] if i >= 1 else None],
                         mk_mix(CMK, 1), inc="ev")
            ev_prev = None
            for oc in range(NDC):
                mmk = p.step("tensor",
                             [mxk, wtok[0], ev_prev,
                              evo_fin[i - 1] if (i >= 1 and oc < 2) else None],
                             proj(0, oc), inc="mm")
                ev_prev = p.step("vector", [mmk],
                                 lambda v, _oc=oc: v.tensor_copy(
                                     kf[:, _oc, :], pp[:, _oc % 2, 0:TC]),
                                 inc="ev")
            mxv = p.step("vector", [ev_prev], mk_mix(CMV, 2), inc="ev")
            for oc in range(NDC):
                mmv = p.step("tensor", [mxv, wtok[1], ev_prev], proj(1, oc),
                             inc="mm")
                ev_prev = p.step("vector", [mmv],
                                 lambda v, _oc=oc: v.tensor_copy(
                                     vf[:, _oc, :], pp[:, _oc % 2, 0:TC]),
                                 inc="ev")
            mxr = p.step("vector", [ev_prev], mk_mix(CMR, 3), inc="ev")
            sgm = p.step("scalar", [mxr],
                         lambda s: s.activation(mx[:, :, :], mx[:, :, :],
                                                AF.Sigmoid),
                         inc="es")
            ev_prev2 = None
            for oc in range(NDC):
                mmr = p.step("tensor", [sgm, wtok[2], ev_prev2], proj(2, oc),
                             inc="mm")
                ev_prev2 = p.step("vector", [mmr],
                                  lambda v, _oc=oc: v.tensor_copy(
                                      rr[:, _oc, :], pp[:, _oc % 2, 0:TC]),
                                  inc="ev")
            mix_done[i] = ev_prev2
            if "wk_dbg" in b.dram and i == 0:
                wdb = b.dram["wk_dbg"].ap().rearrange("(c p) t -> p c t", p=128)
                p.step("sync", [mix_done[i]],
                       lambda sy: sy.dma_start(out=wdb[:, :, 0:TC],
                                               in_=kf[:, :, :]),
                       inc="qs", amt=16)
                p.step("sync", [mix_done[i]],
                       lambda sy: sy.dma_start(out=wdb[:, :, TC:2 * TC],
                                               in_=vf[:, :, :]),
                       inc="qs", amt=16)
                p.step("sync", [mix_done[i]],
                       lambda sy: sy.dma_start(out=wdb[:, :, 2 * TC:3 * TC],
                                               in_=rr[:, :, :]),
                       inc="qs", amt=16)

            def pre_a(v):
                for dc in range(NDC):
                    v.tensor_scalar(qf[:, dc, :], kf[:, dc, :],
                                    c5[:, dc, CU:CU + 1],
                                    c5[:, dc, CW:CW + 1], ALU.add, ALU.max)
                for dc in range(NDC):
                    v.tensor_scalar(kf[:, dc, :], kf[:, dc, :],
                                    c5[:, dc, CU:CU + 1], None, ALU.add)
                ins = None
                for dc in range(NDC):
                    ins = v.tensor_sub(kf[:, dc, :], kf[:, dc, :],
                                       qf[:, dc, :])
                return ins

            t_pre = p.step("vector", [mix_done[i]], pre_a, inc="ev")

            def acts(s):
                for dc in range(NDC):
                    s.activation(qf[:, dc, :], qf[:, dc, :], AF.Exp,
                                 bias=cx[:, dc, 0:1], scale=-1.0)
                ins = None
                for dc in range(NDC):
                    ins = s.activation(kf[:, dc, :], kf[:, dc, :], AF.Exp)
                return ins

            t_act = p.step("scalar", [t_pre], acts, inc="es")

            def scans(v, _i=i):
                for dc in range(NDC):
                    v.tensor_mul(vf[:, dc, :], kf[:, dc, :], vf[:, dc, :])
                for dc in range(NDC):
                    v.tensor_tensor_scan(aa[:, dc, :], qf[:, dc, :],
                                         vf[:, dc, :],
                                         0.0 if _i == 0 else cab[:, dc, 0:1],
                                         ALU.mult, ALU.add)
                for dc in range(NDC):
                    v.tensor_tensor_scan(bbf[:, dc, :], qf[:, dc, :],
                                         kf[:, dc, :],
                                         0.0 if _i == 0 else cab[:, dc, 1:2],
                                         ALU.mult, ALU.add)
                for dc in range(NDC):
                    v.tensor_copy(cab[:, dc, 0:1], aa[:, dc, TC - 1:TC])
                    v.tensor_copy(cab[:, dc, 1:2], bbf[:, dc, TC - 1:TC])
                for dc in range(NDC):
                    v.reciprocal(bbf[:, dc, :], bbf[:, dc, :])
                for dc in range(NDC):
                    v.tensor_mul(aa[:, dc, :], aa[:, dc, :], bbf[:, dc, :])
                ins = None
                for dc in range(NDC):
                    ins = v.tensor_mul(wkvr[:, dc, :], aa[:, dc, :],
                                       rr[:, dc, :])
                return ins

            scan_done[i] = p.step("vector", [t_act, cxt], scans, inc="ev")
            if "wk_dbg" in b.dram and i == 0:
                wdb = b.dram["wk_dbg"].ap().rearrange("(c p) t -> p c t", p=128)
                p.step("sync", [scan_done[i]],
                       lambda sy: sy.dma_start(out=wdb[:, :, 3 * TC:4 * TC],
                                               in_=aa[:, :, :]),
                       inc="qs", amt=16)

            ev_o = None
            st_prev = st_all[i - 1] if i >= 1 else None
            st_last = None
            for oc in range(NDC):
                def projo(t, _oc=oc):
                    ins = None
                    for ic in range(NDC):
                        ins = t.matmul(pp[:, _oc % 2, 0:TC],
                                       W[:, 3, ic, ts(_oc, 128)],
                                       wkvr[:, ic, :], start=(ic == 0),
                                       stop=(ic == NDC - 1))
                    return ins
                mmo = p.step("tensor", [scan_done[i], wtok[3], ev_o], projo,
                             inc="mm")
                lx = p.step("sync", [st_prev],
                            lambda sy, _i=i, _oc=oc: sy.dma_start(
                                out=xr[:, _oc, :],
                                in_=resid[:, _oc, ts(_i, TC)]),
                            inc="qs", amt=16)
                ev_o = p.step("vector", [mmo, lx, botok],
                              lambda v, _oc=oc: v.scalar_tensor_tensor(
                                  xr[:, _oc, :], pp[:, _oc % 2, 0:TC],
                                  boc[:, _oc, 0:1], xr[:, _oc, :],
                                  ALU.add, ALU.add),
                              inc="ev")
                st_last = p.step("sync", [ev_o],
                                 lambda sy, _i=i, _oc=oc: sy.dma_start(
                                     out=resid[:, _oc, ts(_i, TC)],
                                     in_=xr[:, _oc, :]),
                                 inc="qs", amt=16)
            st_all[i] = st_last
            evo_fin[i] = ev_o

        p.step("sync", [st_all[NT - 1]], lambda sy: sy.nop())
        p.flush()
    b.barrier()


def _leadcopy(v, lnc, bi, TC):
    v.tensor_copy(lnc[:, bi, :, 0:1], lnc[:, 1 - bi, :, TC:TC + 1])
    return v.drain()


def stage_rwkv(pre, w):
    out = {}
    for nm, key in [("WkT", "Wk"), ("WvT", "Wv"), ("WrT", "Wr"), ("WoT", "Wo")]:
        out[pre + nm] = np.ascontiguousarray(w[key].T)
    cols = np.zeros((128, NDC, 128), np.float32)
    for j, k in enumerate(["u", "w", "tmk", "tmv", "tmr", "bo"]):
        cols[:, :, j] = col_of(w[k])[:, :, 0]
    out[pre + "cols5"] = np.ascontiguousarray(cols)
    return out


def rwkv_dram_specs(pre):
    return {
        pre + "WkT": ([D, D], F32), pre + "WvT": ([D, D], F32),
        pre + "WrT": ([D, D], F32), pre + "WoT": ([D, D], F32),
        pre + "cols5": ([128, NDC, 128], F32),
    }


def emit_ffn_moe(b, pre, mode, NF):
    nc = b.nc
    resid = b.dram["resid"].ap().rearrange("(c p) t -> p c t", p=128)
    lnd = b.dram["ln_dram"].ap().rearrange("(c p) t -> p c t", p=128)
    xnd = b.dram["xn_dram"].ap().rearrange("(c p) t -> p c t", p=128)
    oacc = b.dram["oacc_dram"].ap().rearrange("(c p) t -> p c t", p=128)
    NE = 8
    TCH = 512
    NTC = T // TCH
    F = NF * 128
    xn_src = "xn_dram" if mode == "l0" else "ln_dram"
    xnp = b.dram[xn_src].ap().rearrange("(c p) t -> p c t", p=128)

    with contextlib.ExitStack() as _xs:
        sm = _xs.enter_context(nc.sbuf_tensor("fm_sm", [128, NDC, 128], F32))
        b1c = _xs.enter_context(nc.sbuf_tensor("fm_b1c", [128, NE, 128], F32))
        rb1 = _xs.enter_context(nc.sbuf_tensor("fm_rb1", [128, NE, 128], F32))
        rb1b = _xs.enter_context(nc.sbuf_tensor("fm_rb1b", [128, NE, 128], BF16))
        afm = _xs.enter_context(nc.sbuf_tensor("fm_afm", [8, T], F32))
        srow = _xs.enter_context(nc.sbuf_tensor("fm_srow", [1, T], F32))
        cps = _xs.enter_context(nc.sbuf_tensor("fm_cps", [128, NDC, NE], F32))
        ctot = _xs.enter_context(nc.sbuf_tensor("fm_ctot", [128, NDC, 1], F32))
        ones8 = _xs.enter_context(nc.sbuf_tensor("fm_ones8", [8, 1], BF16))
        p0 = P(b)
        smtok = p0.step("gpsimd", [],
                        lambda g: g.dma_start(out=sm[:, :, :],
                                              in_=b.dram[pre + "smalls"].ap()),
                        inc="qg", amt=16)
        b1tok = p0.step("gpsimd", [],
                        lambda g: g.dma_start(out=b1c[:, :, :],
                                              in_=b.dram[pre + "b1c"].ap()),
                        inc="qg", amt=16)
        p0.step("vector", [], lambda v: v.memset(ones8[:, :], 1.0), inc="ev")
        rbt = p0.step("vector", [b1tok],
                      lambda v: (v.tensor_scalar_max(rb1[:, :, :], b1c[:, :, :],
                                                     0.0),
                                 v.drain(),
                                 v.tensor_copy(rb1b[:, :, :], rb1[:, :, :]),
                                 v.drain())[-1],
                      inc="ev")
        p0.flush()

        with contextlib.ExitStack() as _xs:
            wrs = _xs.enter_context(nc.sbuf_tensor("fr_wr", [128, NDC, 128], BF16))
            wrsf = _xs.enter_context(nc.sbuf_tensor("fr_wrf", [128, NDC, 128], F32))
            brs = _xs.enter_context(nc.sbuf_tensor("fr_br", [8, 64], F32))
            xnb = _xs.enter_context(nc.sbuf_tensor("fr_xn", [128, 2, NDC, TCH], BF16))
            lgr = _xs.enter_context(nc.sbuf_tensor("fr_lgr", [8, T], F32))
            lgtm = _xs.enter_context(nc.sbuf_tensor("fr_tm", [128, 32, 8], F32))
            Ptm = _xs.enter_context(nc.sbuf_tensor("fr_P", [128, 32, 8], F32))
            t1m = _xs.enter_context(nc.sbuf_tensor("fr_t1", [128, 32, 8], F32))
            t2m = _xs.enter_context(nc.sbuf_tensor("fr_t2", [128, 32, 8], F32))
            ssum = _xs.enter_context(nc.sbuf_tensor("fr_ssum", [128, 32], F32))
            m1t = _xs.enter_context(nc.sbuf_tensor("fr_m1", [128, 32], F32))
            m2t = _xs.enter_context(nc.sbuf_tensor("fr_m2", [128, 32], F32))
            atm = _xs.enter_context(nc.sbuf_tensor("fr_atm", [128, 32, 8], F32))
            psL = _xs.enter_context(nc.psum_tensor("fr_psL", [8, 512], F32))
            psT = _xs.enter_context(nc.psum_tensor("fr_psT", [128, 512], F32))
            ptb0 = _xs.enter_context(nc.psum_tensor("fr_ptb0", [8, 128], F32))
            ptb1 = _xs.enter_context(nc.psum_tensor("fr_ptb1", [8, 128], F32))
            psR = _xs.enter_context(nc.psum_tensor("fr_psR", [1, 512], F32))
            p = P(b)
            wrtok0 = p.step("gpsimd", [],
                            lambda g: g.dma_start(
                                out=wrsf[:, :, :],
                                in_=b.dram[pre + "Wrpad"].ap()
                                .rearrange("(c p) e -> p c e", p=128)),
                            inc="qg", amt=16)
            wrtok = p.step("vector", [wrtok0],
                           lambda v: v.tensor_copy(wrs[:, :, :],
                                                   wrsf[:, :, :]),
                           inc="ev")
            brtok = p.step("gpsimd", [],
                           lambda g: g.dma_start(out=brs[:, :],
                                                 in_=b.dram[pre + "brrow"].ap()),
                           inc="qg", amt=16)
            lg_done = [None] * NTC
            ld_x = [None] * NTC
            for i in range(NTC):
                bi = i % 2
                ld_x[i] = p.step("sync", [lg_done[i - 2] if i >= 2 else None],
                                 lambda sy, _i=i, _bi=bi: sy.dma_start(
                                     out=xnb[:, _bi, :, :],
                                     in_=xnp[:, :, ts(_i, TCH)]),
                                 inc="qs", amt=16)

                def rmm(t, _bi=bi):
                    ins = None
                    for ic in range(NDC):
                        ins = t.matmul(psL[:, :], wrs[:, ic, 0:8],
                                       xnb[:, _bi, ic, :],
                                       start=(ic == 0), stop=(ic == NDC - 1))
                    return ins

                mmt = p.step("tensor", [ld_x[i], wrtok,
                                        lg_done[i - 1] if i >= 1 else None],
                             rmm, inc="mm")
                lg_done[i] = p.step("scalar", [mmt, brtok],
                                    lambda s, _i=i: s.activation(
                                        lgr[:, ts(_i, TCH)], psL[:, :],
                                        AF.Identity, bias=brs[:, 0:1]),
                                    inc="es", fence=True)
            trp = [None] * 32
            for j in range(32):
                trp[j] = p.step("tensor",
                                [lg_done[(j * 128) // TCH],
                                 lg_done[((j + 1) * 128 - 1) // TCH]],
                                lambda t, _j=j: t.transpose(
                                    psT[:, (_j % 64) * 8:(_j % 64) * 8 + 8],
                                    lgr[:, ts(_j, 128)],
                                    b.consts["ident_f32"][0:8, 0:8]),
                                inc="mm")
            tev = p.step("vector", [trp[31]],
                         lambda v: v.tensor_copy(lgtm[:, :, :],
                                                 psT[:, 0:256]),
                         inc="ev")
            pexp = p.step("scalar", [tev],
                          lambda s: s.activation(Ptm[:, :, :], lgtm[:, :, :],
                                                 AF.Exp),
                          inc="es")

            def rmath(v):
                v.tensor_reduce(ssum[:, :], Ptm[:, :, :], mybir.AxisListType.X,
                                ALU.add)
                v.tensor_reduce(m1t[:, :], Ptm[:, :, :], mybir.AxisListType.X,
                                ALU.max)
                v.drain()
                v.tensor_tensor(t1m[:, :, :], Ptm[:, :, :],
                                m1t[:, :].broadcast_to([128, 32, 8]),
                                ALU.is_equal)
                v.drain()
                v.tensor_mul(t2m[:, :, :], Ptm[:, :, :], t1m[:, :, :])
                v.drain()
                v.tensor_sub(t2m[:, :, :], Ptm[:, :, :], t2m[:, :, :])
                v.drain()
                v.tensor_reduce(m2t[:, :], t2m[:, :, :], mybir.AxisListType.X,
                                ALU.max)
                v.drain()
                v.tensor_tensor(t2m[:, :, :], t2m[:, :, :],
                                m2t[:, :].broadcast_to([128, 32, 8]),
                                ALU.is_equal)
                v.reciprocal(ssum[:, :], ssum[:, :])
                v.drain()
                v.tensor_tensor(t1m[:, :, :], t1m[:, :, :],
                                m1t[:, :].broadcast_to([128, 32, 8]), ALU.mult)
                v.tensor_tensor(t2m[:, :, :], t2m[:, :, :],
                                m2t[:, :].broadcast_to([128, 32, 8]), ALU.mult)
                v.drain()
                v.tensor_add(t1m[:, :, :], t1m[:, :, :], t2m[:, :, :])
                v.drain()
                return v.tensor_tensor(atm[:, :, :], t1m[:, :, :],
                                       ssum[:, :].broadcast_to([128, 32, 8]),
                                       ALU.mult)

            rmt = p.step("vector", [pexp], rmath, inc="ev", fence=True)
            if mode == "l2":
                def rmath2(v):
                    v.tensor_add(m1t[:, :], m1t[:, :], m2t[:, :])
                    v.drain()
                    v.reciprocal(m1t[:, :], m1t[:, :])
                    v.drain()
                    return v.tensor_tensor(atm[:, :, :], t1m[:, :, :],
                                           m1t[:, :].broadcast_to([128, 32, 8]),
                                           ALU.mult)
                rmt = p.step("vector", [rmt], rmath2, inc="ev", fence=True)

            prev_ev = None
            for j in range(32):
                tr2 = p.step("tensor", [rmt, prev_ev],
                             lambda t, _j=j: t.transpose(
                                 (ptb0 if _j % 2 == 0 else ptb1)[:, :],
                                 atm[:, _j, :], b.consts["ident_f32"][:, :]),
                             inc="mm")
                prev_ev = p.step("vector", [tr2],
                                 lambda v, _j=j: v.tensor_copy(
                                     afm[:, ts(_j, 128)],
                                     (ptb0 if _j % 2 == 0 else ptb1)[:, :]),
                                 inc="ev", fence=(_j == 31) if False else (j == 31))
            prev_sv = prev_ev
            for i in range(NTC):
                msr = p.step("tensor", [prev_sv],
                             lambda t, _i=i: t.matmul(
                                 psR[:, :], b.consts["ones8col"][:, 0:1],
                                 afm[:, ts(_i, TCH)], start=True, stop=True),
                             inc="mm")
                prev_sv = p.step("vector", [msr],
                                 lambda v, _i=i: v.tensor_copy(
                                     srow[:, ts(_i, TCH)], psR[:, :]),
                                 inc="ev")
            if "lgtm_dbg" in b.dram:
                lc = p.step("vector", [rmt],
                            lambda v: v.tensor_copy(t2m[:, :, :],
                                                    lgtm[:, :, :]),
                            inc="ev")
                p.step("sync", [lc],
                       lambda sy: sy.dma_start(
                           out=b.dram["lgtm_dbg"].ap(),
                           in_=t2m[:, :, :].rearrange("p a b -> p (a b)")),
                       inc="qs", amt=16)
            if "lgr_dbg" in b.dram:
                p.step("sync", [rmt],
                       lambda sy: sy.dma_start(out=b.dram["lgr_dbg"].ap(),
                                               in_=lgr[:, :]),
                       inc="qs", amt=16)
                atc = p.step("vector", [rmt],
                             lambda v: v.tensor_copy(
                                 t1m[:, :, :], atm[:, :, :]),
                             inc="ev")
                p.step("sync", [atc],
                       lambda sy: sy.dma_start(
                           out=b.dram["atm_dbg"].ap(),
                           in_=t1m[:, :, :].rearrange("p a b -> p (a b)")),
                       inc="qs", amt=16)
            if "afm_dbg" in b.dram:
                p.step("sync", [prev_sv],
                       lambda sy: sy.dma_start(out=b.dram["afm_dbg"].ap(),
                                               in_=afm[:, :]),
                       inc="qs", amt=16)
                p.step("sync", [prev_sv],
                       lambda sy: sy.dma_start(out=b.dram["srow_dbg"].ap(),
                                               in_=srow[:, :]),
                       inc="qs", amt=16)
            p.step("sync", [prev_sv], lambda sy: sy.nop())
            p.flush()
        b.barrier()

        hd = b.dram["h_dram"].ap()[0:NE * NF * 128, :].rearrange(
            "(e f p) t -> e f p t", p=128, f=NF)
        hd_r = b.dram["h_dram"].ap()[0:NE * NF * 128, :].rearrange(
            "(e f p) t -> e p f t", p=128, f=NF)
        with contextlib.ExitStack() as _xs:
            xnr = _xs.enter_context(nc.sbuf_tensor("fa_xn", [128, NDC, T], BF16))
            w1b = _xs.enter_context(nc.sbuf_tensor("fa_w1", [128, NDC, NF * 64], BF16))
            abce = _xs.enter_context(nc.sbuf_tensor("fa_abc", [128, NTC, TCH], BF16))
            ht = _xs.enter_context(nc.sbuf_tensor("fa_ht", [128, 2, TCH], BF16))
            hstage = _xs.enter_context(nc.sbuf_tensor("fa_hs", [128, 2, T], BF16))
            psA = _xs.enter_context(nc.psum_tensor("fa_psA", [128, 2, 512], F32))
            psB = _xs.enter_context(nc.psum_tensor("fa_psB", [128, 512], F32))
            p = P(b)
            xntok = p.step("gpsimd", [],
                           lambda g: g.dma_start(out=xnr[:, :, :],
                                                 in_=xnp[:, :, :]),
                           inc="qg", amt=16)
            w1d = b.dram[pre + "W1"].ap()
            hst_tok = [None, None]
            hev_tok = [None, None]
            w1tok = None
            mm1_last = None
            for e in range(NE):
                abq = None
                for i in range(NTC):
                    mab = p.step("tensor",
                                 [abq] + ([hev_tok[0], hev_tok[1]]
                                          if i == 0 else []),
                                 lambda t, _i=i, _e=e: t.matmul(
                                     psB[:, :],
                                     b.consts["sel8"][:, ts(_e, 128)],
                                     afm[:, ts(_i, TCH)], start=True,
                                     stop=True),
                                 inc="mm")
                    abq = p.step("vector", [mab],
                                 lambda v, _i=i: v.tensor_copy(
                                     abce[:, _i, :], psB[:, :]),
                                 inc="ev")
                for fc in range(NF):
                    fb = fc % 2
                    if fc % (NF // 2) == 0:
                        w1tok = p.step(
                            "gpsimd", [mm1_last],
                            lambda g, _e=e, _h=fc // (NF // 2): g.dma_start(
                                out=w1b[:, :, :],
                                in_=w1d[_e].rearrange(
                                    "(c p) f -> p c f", p=128)
                                [:, :, ts(_h, NF * 64)]),
                            inc="qg", amt=16)
                    ev_h = None
                    h1_prev = None
                    evp = [None, None]
                    for i in range(NTC):
                        mm1 = p.step("tensor",
                                     [w1tok, xntok if (e == 0 and fc == 0)
                                      else None, abq if i == 0 else None,
                                      hev_tok[fb] if i == 0 else None,
                                      h1_prev, hst_tok[fb] if i == 0 else None],
                                     lambda t, _fb=fb, _fc=fc, _i=i: [
                                         t.matmul(psA[:, _fb, :],
                                                  w1b[:, ic,
                                                      ts(_fc % (NF // 2), 128)],
                                                  xnr[:, ic, ts(_i, TCH)],
                                                  start=(ic == 0),
                                                  stop=(ic == NDC - 1))
                                         for ic in range(NDC)][-1],
                                     inc="mm")
                        mm1_last = mm1
                        if "psa_dbg" in b.dram and e == 0 and fc == 0 and i == 0:
                            p.step("sync", [abq],
                                   lambda sy: sy.dma_start(
                                       out=b.dram["psa_dbg"].ap(),
                                       in_=abce[:, 0, :]),
                                   inc="qs", amt=16)
                        h1 = p.step("scalar", [mm1, evp[i % 2]],
                                    lambda s, _fb=fb, _e=e, _fc=fc, _i=i:
                                    s.activation(ht[:, _i % 2, :],
                                                 psA[:, _fb, :], AF.Relu,
                                                 bias=b1c[:, _e, _fc:_fc + 1]),
                                    inc="es")
                        h1_prev = h1
                        ev_h = p.step("vector", [h1, rbt],
                                      lambda v, _fb=fb, _e=e, _fc=fc, _i=i:
                                      v.scalar_tensor_tensor(
                                          hstage[:, _fb, ts(_i, TCH)],
                                          ht[:, _i % 2, :],
                                          (rb1[:, _e, _fc:_fc + 1]
                                           if mode == "l0" else 0.0),
                                          abce[:, _i, :],
                                          ALU.subtract, ALU.mult),
                                      inc="ev")
                        evp[i % 2] = ev_h
                    hev_tok[fb] = ev_h
                    hst_tok[fb] = p.step(
                        "sync", [ev_h],
                        lambda sy, _e=e, _fc=fc, _fb=fb: sy.dma_start(
                            out=hd[_e, _fc, :, :], in_=hstage[:, _fb, :]),
                        inc="qs", amt=16)
            p.step("sync", [hst_tok[0], hst_tok[1]], lambda sy: sy.nop())
            p.flush()
        b.barrier()

        w2d = b.dram[pre + "W2"].ap()
        with contextlib.ExitStack() as _xs:
            w2r = _xs.enter_context(nc.sbuf_tensor("fb_w2", [128, NF, D], BF16))
            hbuf = _xs.enter_context(nc.sbuf_tensor("fb_h", [128, 2, NF, TCH], BF16))
            obuf = _xs.enter_context(nc.sbuf_tensor("fb_ob", [128, 2, TCH], F32))
            psO = _xs.enter_context(nc.psum_tensor("fb_psO", [128, 2, 512], F32))
            psC = _xs.enter_context(nc.psum_tensor("fb_psC", [128, 512], F32))
            p = P(b)
            oat = [[None] * NTC for _ in range(NDC)]
            ob_tok = [None, None]
            mml = [None, None]
            cps_ev = None
            for e in range(NE):
                w2tok = None
                for g4 in range(4):
                    w2tok = p.step(
                        "gpsimd",
                        [ob_tok[0], ob_tok[1]] if g4 == 0 else [],
                        lambda g, _e=e, _g=g4: g.dma_start(
                            out=w2r[:, ts(_g, NF // 4), :],
                            in_=w2d[_e].rearrange("(c p) d -> p c d", p=128)
                            [:, ts(_g, NF // 4), :]),
                        inc="qg", amt=16)
                for dc in (range(NDC) if mode == "l0" else []):
                    mc = p.step("tensor", [w2tok, rbt, cps_ev],
                                lambda t, _e=e, _dc=dc: [
                                    t.matmul(psC[:, 0:1],
                                             w2r[:, fc, ts(_dc, 128)],
                                             rb1b[:, _e, fc:fc + 1],
                                             start=(fc == 0),
                                             stop=(fc == NF - 1))
                                    for fc in range(NF)][-1],
                                inc="mm")
                    cps_ev = p.step("vector", [mc],
                                    lambda v, _e=e, _dc=dc: v.tensor_copy(
                                        cps[:, _dc, _e:_e + 1], psC[:, 0:1]),
                                    inc="ev")
                for i in range(NTC):
                    hb = i % 2
                    htok = p.step(
                        "sync", [mml[hb]],
                        lambda sy, _e=e, _i=i, _hb=hb: sy.dma_start(
                            out=hbuf[:, _hb, :, :],
                            in_=hd_r[_e, :, :, ts(_i, TCH)]),
                        inc="qs", amt=16)
                    for dc in range(NDC):
                        db = dc % 2
                        mm2 = p.step(
                            "tensor",
                            [htok, w2tok if (i == 0 and dc == 0) else None,
                             ob_tok[db] if True else None],
                            lambda t, _hb=hb, _db=db, _dc=dc: [
                                t.matmul(psO[:, _db, :],
                                         w2r[:, fc, ts(_dc, 128)],
                                         hbuf[:, _hb, fc, :],
                                         start=(fc == 0),
                                         stop=(fc == NF - 1))
                                for fc in range(NF)][-1],
                            inc="mm")
                        mml[hb] = mm2
                        oev = p.step("vector", [mm2],
                                     lambda v, _db=db: v.tensor_copy(
                                         obuf[:, _db, :], psO[:, _db, :]),
                                     inc="ev")
                        oat[dc][i] = p.step(
                            "gpsimd", [oev, oat[dc][i]],
                            lambda g, _e=e, _i=i, _dc=dc, _db=db: g.dma_start(
                                out=oacc[:, _dc, ts(_i, TCH)],
                                in_=obuf[:, _db, :],
                                **({} if _e == 0 else
                                   {"accum_op": ALU.add})),
                            inc="qg", amt=16)
                        ob_tok[db] = oat[dc][i]
            p.step("gpsimd", [oat[NDC - 1][NTC - 1]], lambda g: g.nop())
            p.flush()
        b.barrier()

        with contextlib.ExitStack() as _xs:
            oab = _xs.enter_context(nc.sbuf_tensor("fc_oa", [128, 2, NDC, TCH], F32))
            lnb = _xs.enter_context(nc.sbuf_tensor("fc_ln", [128, 2, NDC, TCH], BF16))
            xb2 = _xs.enter_context(nc.sbuf_tensor("fc_x", [128, 2, NDC, TCH], F32))
            sbc = _xs.enter_context(nc.sbuf_tensor("fc_sb", [128, 2, TCH], F32))
            b2s = _xs.enter_context(nc.sbuf_tensor("fc_b2s", [128, NDC, 1], F32))
            b2r = _xs.enter_context(nc.sbuf_tensor("fc_b2r", [8, D], F32))
            psS = _xs.enter_context(nc.psum_tensor("fc_ps", [128, 2, 512], F32))
            p = P(b)
            def ctf(v):
                v.tensor_reduce(ctot[:, :, 0], cps[:, :, :],
                                mybir.AxisListType.X, ALU.add)
                v.tensor_reduce(b2s[:, :, 0], sm[:, :, 8:16],
                                mybir.AxisListType.X, ALU.add)
                v.drain()
                if mode == "l0":
                    return v.tensor_add(ctot[:, :, 0], ctot[:, :, 0],
                                        b2s[:, :, 0])
                else:
                    return v.tensor_add(ctot[:, :, 0], ctot[:, :, 0],
                                        b2s[:, :, 0])
            if mode == "l0":
                ctok2 = p.step("vector", [smtok], ctf, inc="ev")
            else:
                b2tok = p.step("gpsimd", [],
                               lambda g: g.dma_start(
                                   out=b2r[:, :],
                                   in_=b.dram[pre + "b2row"].ap()),
                               inc="qg", amt=16)
            wr_fin = [None] * NTC
            cmbs = [None] * NTC
            sbes = [None] * NTC
            pa_bank = [None, None]
            for i in range(NTC):
                bi = i % 2
                lo = p.step("sync", [wr_fin[i - 2] if i >= 2 else None,
                                     cmbs[i - 2] if i >= 2 else None],
                            lambda sy, _i=i, _bi=bi: sy.dma_start(
                                out=oab[:, _bi, :, :],
                                in_=oacc[:, :, ts(_i, TCH)]),
                            inc="qs", amt=16)
                ll = None
                if mode == "l0":
                    ll = p.step("sync", [cmbs[i - 2] if i >= 2 else None],
                                lambda sy, _i=i, _bi=bi: sy.dma_start(
                                    out=lnb[:, _bi, :, :],
                                    in_=lnd[:, :, ts(_i, TCH)]),
                                inc="qs", amt=16)
                lx2 = p.step("sync", [cmbs[i - 2] if i >= 2 else None],
                             lambda sy, _i=i, _bi=bi: sy.dma_start(
                                 out=xb2[:, _bi, :, :],
                                 in_=resid[:, :, ts(_i, TCH)]),
                             inc="qs", amt=16)
                if mode == "l2":
                    adds1 = p.step("vector", [lo, lx2],
                                   lambda v, _bi=bi: [v.tensor_add(
                                       oab[:, _bi, dc, :], oab[:, _bi, dc, :],
                                       xb2[:, _bi, dc, :])
                                       for dc in range(NDC)][-1],
                                   inc="ev", fence=True)
                    cmb = None
                    for dc in range(NDC):
                        db = dc % 2
                        mb2 = p.step("tensor", [b2tok, adds1, pa_bank[db]],
                                     lambda t, _i=i, _db=db, _dc=dc: t.matmul(
                                         psS[:, _db, :],
                                         b2r[0:8, ts(_dc, 128)],
                                         afm[0:8, ts(_i, TCH)],
                                         start=True, stop=True),
                                     inc="mm")
                        cmb = p.step("vector", [mb2],
                                     lambda v, _bi=bi, _db=db, _dc=dc:
                                     v.tensor_add(oab[:, _bi, _dc, :],
                                                  oab[:, _bi, _dc, :],
                                                  psS[:, _db, :]),
                                     inc="ev", fence=(dc == NDC - 1))
                        pa_bank[db] = cmb
                    cmbs[i] = cmb
                    wr_fin[i] = p.step("sync", [cmb],
                                       lambda sy, _i=i, _bi=bi: sy.dma_start(
                                           out=resid[:, :, ts(_i, TCH)],
                                           in_=oab[:, _bi, :, :]),
                                       inc="qs", amt=16)
                    continue
                msb = p.step("tensor", [ctok2, sbes[i - 2] if i >= 2 else None],
                             lambda t, _i=i, _bi=bi: t.matmul(
                                 psS[:, _bi, :], b.consts["ones_rowf"][:, :],
                                 srow[:, ts(_i, TCH)], start=True, stop=True),
                             inc="mm")
                sbe = p.step("vector", [msb],
                             lambda v, _bi=bi: v.tensor_copy(
                                 sbc[:, _bi, :], psS[:, _bi, :]),
                             inc="ev")
                sbes[i] = sbe

                def comb(v, _bi=bi):
                    for dc in range(NDC):
                        v.scalar_tensor_tensor(oab[:, _bi, dc, :],
                                               sbc[:, _bi, :],
                                               ctot[:, dc, 0:1],
                                               oab[:, _bi, dc, :],
                                               ALU.mult, ALU.add)
                    if mode == "l0":
                        for dc in range(NDC):
                            v.tensor_add(oab[:, _bi, dc, :],
                                         oab[:, _bi, dc, :],
                                         lnb[:, _bi, dc, :])
                    ins = None
                    for dc in range(NDC):
                        ins = v.tensor_add(oab[:, _bi, dc, :],
                                           oab[:, _bi, dc, :],
                                           xb2[:, _bi, dc, :])
                    return ins

                cmb = p.step("vector", [lo, ll, lx2, sbe], comb, inc="ev")
                cmbs[i] = cmb
                wr_fin[i] = p.step("sync", [cmb],
                                   lambda sy, _i=i, _bi=bi: sy.dma_start(
                                       out=resid[:, :, ts(_i, TCH)],
                                       in_=oab[:, _bi, :, :]),
                                   inc="qs", amt=16)
            p.step("sync", [wr_fin[NTC - 1]], lambda sy: sy.nop())
            p.flush()
    b.barrier()


def stage_ffn_l0(pre, w):
    out = {}
    out[pre + "W1"] = np.ascontiguousarray(w["W1"])
    out[pre + "W2"] = np.ascontiguousarray(w["W2"])
    wr = np.zeros((D, 128), np.float32)
    wr[:, 0:8] = w["Wr"]
    out[pre + "Wrpad"] = wr
    br = np.zeros((8, 64), np.float32)
    br[:, 0] = w["br"]
    out[pre + "brrow"] = br
    smalls = np.zeros((128, NDC, 128), np.float32)
    smalls[:, :, 0] = col_of(w["gamma"])[:, :, 0]
    smalls[:, :, 1] = col_of(w["beta"])[:, :, 0]
    for e in range(8):
        smalls[:, :, 8 + e] = col_of(w["b2"][e])[:, :, 0]
    out[pre + "smalls"] = smalls
    NF = w["W1"].shape[2] // 128
    b1c = np.zeros((128, 8, 128), np.float32)
    for e in range(8):
        b1c[:, e, 0:NF] = w["b1"][e].reshape(NF, 128).T
    out[pre + "b1c"] = np.ascontiguousarray(b1c)
    return out


def stage_ffn_l2(pre, w):
    out = {}
    out[pre + "W1"] = np.ascontiguousarray(w["W1"])
    out[pre + "W2"] = np.ascontiguousarray(w["W2"])
    wr = np.zeros((D, 128), np.float32)
    wr[:, 0:8] = w["Wr"]
    out[pre + "Wrpad"] = wr
    br = np.zeros((8, 64), np.float32)
    br[:, 0] = w["br"]
    out[pre + "brrow"] = br
    out[pre + "smalls"] = np.zeros((128, NDC, 128), np.float32)
    NF = w["W1"].shape[2] // 128
    b1c = np.zeros((128, 8, 128), np.float32)
    for e in range(8):
        b1c[:, e, 0:NF] = w["b1"][e].reshape(NF, 128).T
    out[pre + "b1c"] = np.ascontiguousarray(b1c)
    out[pre + "b2row"] = np.ascontiguousarray(w["b2"])
    return out


def ffn_l2_dram_specs(pre, NF=16):
    return {
        pre + "W1": ([8, D, NF * 128], F32), pre + "W2": ([8, NF * 128, D], F32),
        pre + "Wrpad": ([D, 128], F32), pre + "brrow": ([8, 64], F32),
        pre + "smalls": ([128, NDC, 128], F32),
        pre + "b1c": ([128, 8, 128], F32),
        pre + "b2row": ([8, D], F32),
    }


def ffn_l0_dram_specs(pre, NF=32):
    return {
        pre + "W1": ([8, D, NF * 128], F32), pre + "W2": ([8, NF * 128, D], F32),
        pre + "Wrpad": ([D, 128], F32), pre + "brrow": ([8, 64], F32),
        pre + "smalls": ([128, NDC, 128], F32),
        pre + "b1c": ([128, 8, 128], F32),
    }


def emit_attn(b, pre, cfg):
    nc = b.nc
    resid = b.dram["resid"].ap().rearrange("(c p) t -> p c t", p=128)
    lnd = b.dram["ln_dram"].ap().rearrange("(c p) t -> p c t", p=128)
    qd = b.dram["q_dram"].ap().rearrange("(c p) t -> p c t", p=128)
    TCH = 512
    NTC = T // TCH
    NH = 16
    causal = cfg.get("causal", False)
    scale = cfg["scale"]

    gbias = cfg.get("gbias", False)
    with contextlib.ExitStack() as xs:
        sm = xs.enter_context(nc.sbuf_tensor("at_sm", [128, NDC, 128], F32))
        Kall = xs.enter_context(nc.sbuf_tensor("at_K", [128, NDC, T], BF16))
        if gbias:
            kar = xs.enter_context(nc.sbuf_tensor("at_kar", [3, T], BF16))
        p0 = P(b)
        smtok = p0.step("gpsimd", [],
                        lambda g: g.dma_start(out=sm[:, :, :],
                                              in_=b.dram[pre + "smalls"].ap()),
                        inc="qg", amt=16)
        p0.flush()

        if cfg.get("rope"):
            ropec_d = b.dram[pre + "ropec"].ap()
            ropes_d = b.dram[pre + "ropes"].ap()
            for kind in ("q", "k"):
                wnm = ("Wq", "Wqsw") if kind == "q" else ("Wk", "Wksw")
                bcol, bcol2 = (1, 4) if kind == "q" else (2, 5)
                with contextlib.ExitStack() as x2:
                    wA = x2.enter_context(
                        nc.sbuf_tensor("ar_w" + kind, [128, NDC, D], BF16))
                    wB = x2.enter_context(
                        nc.sbuf_tensor("ar_s" + kind, [128, NDC, D], BF16))
                    lnb = x2.enter_context(
                        nc.sbuf_tensor("ar_ln" + kind, [128, NDC, TCH], BF16))
                    tq = x2.enter_context(
                        nc.sbuf_tensor("ar_tq" + kind, [128, TCH], F32))
                    tq2 = x2.enter_context(
                        nc.sbuf_tensor("ar_tq2" + kind, [128, TCH], F32))
                    tv1 = x2.enter_context(
                        nc.sbuf_tensor("ar_tv1" + kind, [128, TCH], F32))
                    tv2 = x2.enter_context(
                        nc.sbuf_tensor("ar_tv2" + kind, [128, TCH], F32))
                    rc2 = x2.enter_context(
                        nc.sbuf_tensor("ar_rc" + kind, [128, TCH], F32))
                    rs2 = x2.enter_context(
                        nc.sbuf_tensor("ar_rs" + kind, [128, TCH], F32))
                    if kind == "q":
                        qst = x2.enter_context(
                            nc.sbuf_tensor("ar_qs", [128, NDC, TCH], BF16))
                    psq = x2.enter_context(
                        nc.psum_tensor("ar_psq" + kind, [128, 2, 512], F32))
                    psq2 = x2.enter_context(
                        nc.psum_tensor("ar_psq2" + kind, [128, 2, 512], F32))
                    p = P(b)
                    wAt = p.step("gpsimd", [],
                                 lambda g, _n=wnm[0]: g.dma_start(
                                     out=wA[:, :, :],
                                     in_=b.dram[pre + _n].ap()
                                     .rearrange("(c p) o -> p c o", p=128)),
                                 inc="qg", amt=16)
                    wBt = p.step("gpsimd", [],
                                 lambda g, _n=wnm[1]: g.dma_start(
                                     out=wB[:, :, :],
                                     in_=b.dram[pre + _n].ap()
                                     .rearrange("(c p) o -> p c o", p=128)),
                                 inc="qg", amt=16)
                    qw = [None] * NTC
                    v3l = [None] * NTC
                    for i in range(NTC):
                        mml = None
                        ldt = p.step("sync",
                                     [v3l[i - 1] if i >= 1 else None],
                                     lambda sy, _i=i: sy.dma_start(
                                         out=lnb[:, :, :],
                                         in_=lnd[:, :, ts(_i, TCH)]),
                                     inc="qs", amt=16)
                        ldr = p.step("sync",
                                     [v3l[i - 1] if i >= 1 else None],
                                     lambda sy, _i=i: sy.dma_start(
                                         out=rc2[:, :],
                                         in_=ropec_d[:, ts(_i, TCH)]),
                                     inc="qs", amt=16)
                        ldr2 = p.step("sync",
                                      [v3l[i - 1] if i >= 1 else None],
                                      lambda sy, _i=i: sy.dma_start(
                                          out=rs2[:, :],
                                          in_=ropes_d[:, ts(_i, TCH)]),
                                      inc="qs", amt=16)
                        s1p = [None, None]
                        s2p = [None, None]
                        v1p = None
                        v2p = None
                        v3p = None
                        for oc in range(NDC):
                            ob = oc % 2
                            mmq = p.step("tensor",
                                         [ldt, wAt, s1p[ob]],
                                         lambda t, _oc=oc, _ob=ob: [
                                             t.matmul(psq[:, _ob, :],
                                                      wA[:, ic, ts(_oc, 128)],
                                                      lnb[:, ic, :],
                                                      start=(ic == 0),
                                                      stop=(ic == NDC - 1))
                                             for ic in range(NDC)][-1],
                                         inc="mm")
                            mmq2 = p.step("tensor",
                                          [wBt, s2p[ob]],
                                          lambda t, _oc=oc, _ob=ob: [
                                              t.matmul(psq2[:, _ob, :],
                                                       wB[:, ic, ts(_oc, 128)],
                                                       lnb[:, ic, :],
                                                       start=(ic == 0),
                                                       stop=(ic == NDC - 1))
                                              for ic in range(NDC)][-1],
                                          inc="mm")
                            mml = mmq2
                            s1 = p.step("scalar", [mmq, smtok, v1p],
                                        lambda s, _oc=oc, _ob=ob:
                                        s.activation(tq[:, :], psq[:, _ob, :],
                                                     AF.Identity,
                                                     bias=sm[:, _oc,
                                                             bcol:bcol + 1]),
                                        inc="es")
                            s1p[ob] = s1
                            s2 = p.step("scalar", [mmq2, v2p],
                                        lambda s, _oc=oc, _ob=ob:
                                        s.activation(tq2[:, :],
                                                     psq2[:, _ob, :],
                                                     AF.Identity,
                                                     bias=sm[:, _oc,
                                                             bcol2:bcol2 + 1]),
                                        inc="es")
                            s2p[ob] = s2
                            v1 = p.step("vector", [s1, ldr, v3p],
                                        lambda v: v.tensor_mul(
                                            tv1[:, :], tq[:, :], rc2[:, :]),
                                        inc="ev")
                            v1p = v1
                            v2 = p.step("vector", [s2, ldr2],
                                        lambda v: v.tensor_mul(
                                            tv2[:, :], tq2[:, :], rs2[:, :]),
                                        inc="ev")
                            v2p = v2
                            if kind == "q":
                                outap = qst[:, oc, :]
                            else:
                                outap = Kall[:, oc, ts(i, TCH)]
                            v3 = p.step("vector",
                                        [v2, qw[i - 1] if (kind == "q"
                                                           and oc == 0
                                                           and i >= 1)
                                         else None],
                                        lambda v, _o=outap: (
                                            v.drain(),
                                            v.tensor_add(_o, tv1[:, :],
                                                         tv2[:, :]),
                                            v.drain())[-1],
                                        inc="ev")
                            v3p = v3
                        v3l[i] = v3p
                        if kind == "q":
                            qw[i] = p.step("sync", [v3p],
                                           lambda sy, _i=i: sy.dma_start(
                                               out=qd[:, :, ts(_i, TCH)],
                                               in_=qst[:, :, :]),
                                           inc="qs", amt=16)
                    if kind == "q":
                        p.step("sync", [qw[NTC - 1]], lambda sy: sy.nop())
                    else:
                        p.step("vector", [v3l[NTC - 1]],
                               lambda v: v.engine_nop())
                    p.flush()
                b.barrier()

        rms = cfg.get("rms", False)
        gate = cfg.get("gate", False)
        if not cfg.get("rope"):
          with contextlib.ExitStack() as x2:
            wq = x2.enter_context(nc.sbuf_tensor("at_wq", [128, NDC, D], BF16))
            wk = x2.enter_context(nc.sbuf_tensor("at_wk", [128, NDC, D], BF16))
            lnb = x2.enter_context(nc.sbuf_tensor("at_ln", [128, 1, NDC, TCH], BF16))
            qst = x2.enter_context(nc.sbuf_tensor("at_qs", [128, 1, NDC, TCH], BF16))
            psq = x2.enter_context(nc.psum_tensor("at_psq", [128, 2, 512], F32))
            psk = x2.enter_context(nc.psum_tensor("at_psk", [128, 2, 512], F32))
            if rms:
                tqq = x2.enter_context(nc.sbuf_tensor("at_tqq", [128, TCH], F32))
                tkk = x2.enter_context(nc.sbuf_tensor("at_tkk", [128, TCH], F32))
                sqq = x2.enter_context(nc.sbuf_tensor("at_sqq", [128, TCH], BF16))
                sqf = x2.enter_context(nc.sbuf_tensor("at_sqf", [128, TCH], F32))
                rl1 = x2.enter_context(nc.sbuf_tensor("at_rl1", [128, 8], F32))
                rst = x2.enter_context(nc.sbuf_tensor("at_rst", [128, 8], F32))
                rstf = x2.enter_context(nc.sbuf_tensor("at_rsf", [8, 128], F32))
                psS2 = x2.enter_context(nc.psum_tensor("at_ps2", [128, 8], F32))
                psT2 = x2.enter_context(nc.psum_tensor("at_pT2", [8, 128], F32))
                psR2 = x2.enter_context(nc.psum_tensor("at_pR2", [128, 512], F32))
            if gate:
                wgsf = x2.enter_context(nc.sbuf_tensor("at_wgf", [128, NDC, 128], F32))
                wgs = x2.enter_context(nc.sbuf_tensor("at_wgs", [128, NDC, 128], BF16))
                gsig = x2.enter_context(nc.sbuf_tensor("at_gsg", [16, 512], F32))
                psG = x2.enter_context(nc.psum_tensor("at_psG", [16, 512], F32))
                gdram = b.dram[pre + "gdram"].ap()
            p = P(b)
            wqt = p.step("gpsimd", [],
                         lambda g: g.dma_start(
                             out=wq[:, :, :],
                             in_=b.dram[pre + "Wq"].ap()
                             .rearrange("(c p) o -> p c o", p=128)),
                         inc="qg", amt=16)
            wkt = p.step("gpsimd", [],
                         lambda g: g.dma_start(
                             out=wk[:, :, :],
                             in_=b.dram[pre + "Wk"].ap()
                             .rearrange("(c p) o -> p c o", p=128)),
                         inc="qg", amt=16)
            if gate:
                wgt0 = p.step("gpsimd", [],
                              lambda g: g.dma_start(
                                  out=wgsf[:, :, :],
                                  in_=b.dram[pre + "Wgpad"].ap()
                                  .rearrange("(c p) o -> p c o", p=128)),
                              inc="qg", amt=16)
                wgt = p.step("vector", [wgt0],
                             lambda v: v.tensor_copy(wgs[:, :, :],
                                                     wgsf[:, :, :]),
                             inc="ev", fence=True)
            ld = [None] * NTC
            qw = [None] * NTC
            gw = [None] * NTC
            kev_last = [None] * NTC
            sg_prev = None

            def emit_rms(side_mm, side_ps, side_ob, side_oc, tdst, sq_t, deps):
                s_b = p.step("scalar", [side_mm, smtok, deps["tdst_war"]],
                             lambda s, _oc=side_oc, _ob=side_ob,
                             _bc=deps["bcol"]: s.activation(
                                 tdst[:, :], side_ps[:, _ob, :], AF.Identity,
                                 bias=sm[:, _oc, _bc:_bc + 1]),
                             inc="es")
                s_sq0 = p.step("scalar", [side_mm],
                               lambda s, _oc=side_oc, _ob=side_ob,
                               _bc=deps["bcol"]: s.activation(
                                   sqf[:, :], side_ps[:, _ob, :], AF.Square,
                                   bias=sm[:, _oc, _bc:_bc + 1]),
                               inc="es")
                s_sq = p.step("vector", [s_sq0],
                              lambda v: v.tensor_copy(sq_t[:, :], sqf[:, :]),
                              inc="ev", fence=True)

                def f_ss(t):
                    ins = None
                    for s4 in range(4):
                        for hh in range(2):
                            c = s4 * 2 + hh
                            ins = t.matmul(psS2[:, c:c + 1],
                                           sq_t[hh * 64:(hh + 1) * 64,
                                                ts(s4, 128)],
                                           b.consts["ones_col"]
                                           [hh * 64:(hh + 1) * 64, 0:1],
                                           start=True, stop=True)
                    return ins
                mm_ss = p.step("tensor", [s_sq, deps["psS2w"]], f_ss,
                               inc="mm")
                s_ln = p.step("scalar", [mm_ss],
                              lambda s: s.activation(
                                  rl1[:, :], psS2[:, :], AF.Ln,
                                  bias=b.consts["epscol"][:, 1:2],
                                  scale=1.0 / 64.0),
                              inc="es", fence=True)
                s_ex = p.step("scalar", [s_ln],
                              lambda s: s.activation(rst[:, :], rl1[:, :],
                                                     AF.Exp, scale=-0.5),
                              inc="es", fence=True)
                mm_tr = p.step("tensor", [s_ex, deps["psTw"]],
                               lambda t: t.transpose(
                                   psT2[:, :], rst[:, :],
                                   b.consts["ident_f32"][:, :]),
                               inc="mm")
                v_tr = p.step("vector", [mm_tr],
                              lambda v: v.tensor_copy(rstf[:, :], psT2[:, :]),
                              inc="ev", fence=True)

                def f_bc(t):
                    ins = None
                    for s4 in range(4):
                        ins = t.matmul(psR2[:, ts(s4, 128)],
                                       b.consts["bc2c"][:, ts(s4, 128)],
                                       rstf[:, :], start=True, stop=True)
                    return ins
                mm_bc = p.step("tensor", [v_tr, deps["psRw"]], f_bc,
                               inc="mm")
                stq = p.step("vector", [mm_bc, s_b],
                             lambda v, _gc=deps["gaincol"],
                             _o=deps["outap"]: v.scalar_tensor_tensor(
                                 _o, tdst[:, :], sm[:, 0, _gc:_gc + 1],
                                 psR2[:, :], ALU.mult, ALU.mult),
                             inc="ev", fence=True)
                return stq

            dq = {"psS2w": None, "psTw": None, "psRw": None,
                  "tdst_war": None}
            dk = {"psS2w": None, "psTw": None, "psRw": None,
                  "tdst_war": None}
            sln_q = [None]
            for i in range(NTC):
                bi = 0
                ld[i] = p.step("sync",
                               [qw[i - 1] if i >= 1 else None,
                                kev_last[i - 1] if i >= 1 else None],
                               lambda sy, _i=i, _bi=bi: sy.dma_start(
                                   out=lnb[:, _bi, :, :],
                                   in_=lnd[:, :, ts(_i, TCH)]),
                               inc="qs", amt=16)
                if gate:
                    mmg = p.step("tensor", [ld[i], wgt, sg_prev],
                                 lambda t, _bi=bi: [
                                     t.matmul(psG[:, :],
                                              wgs[:, ic, 0:16],
                                              lnb[:, _bi, ic, :],
                                              start=(ic == 0),
                                              stop=(ic == NDC - 1))
                                     for ic in range(NDC)][-1],
                                 inc="mm")
                    sg = p.step("scalar", [mmg, smtok,
                                           gw[i - 1] if i >= 1 else None],
                                lambda s: s.activation(
                                    gsig[:, :], psG[:, :], AF.Sigmoid,
                                    bias=sm[0:16, 0, 6:7]),
                                inc="es", fence=True)
                    sg_prev = sg
                    gw[i] = p.step("sync", [sg],
                                   lambda sy, _i=i: sy.dma_start(
                                       out=gdram[:, ts(_i, TCH)],
                                       in_=gsig[:, :]),
                                   inc="qs", amt=16)
                qe_prev = None
                ke_prev = None
                for oc in range(NDC):
                    ob = oc % 2
                    mmq = p.step("tensor",
                                 [ld[i], wqt, qe_prev],
                                 lambda t, _bi=bi, _oc=oc, _ob=ob: [
                                     t.matmul(psq[:, _ob, :],
                                              wq[:, ic, ts(_oc, 128)],
                                              lnb[:, _bi, ic, :],
                                              start=(ic == 0),
                                              stop=(ic == NDC - 1))
                                     for ic in range(NDC)][-1],
                                 inc="mm")
                    if rms:
                        dq.update(bcol=1, gaincol=7,
                                  outap=qst[:, 0, oc, :])
                        stq = emit_rms(mmq, psq, ob, oc, tqq, sqq, dq)
                        dq["tdst_war"] = stq
                        dq["psRw"] = stq
                        qe_prev = stq
                    else:
                        qe_prev = p.step(
                            "scalar", [mmq, smtok],
                            lambda s, _bi=bi, _oc=oc, _ob=ob: s.activation(
                                qst[:, 0, _oc, :], psq[:, _ob, :],
                                AF.Identity, bias=sm[:, _oc, 1:2]),
                            inc="es")
                    mmk = p.step("tensor",
                                 [ld[i], wkt, ke_prev],
                                 lambda t, _bi=bi, _oc=oc, _ob=ob: [
                                     t.matmul(psk[:, _ob, :],
                                              wk[:, ic, ts(_oc, 128)],
                                              lnb[:, _bi, ic, :],
                                              start=(ic == 0),
                                              stop=(ic == NDC - 1))
                                     for ic in range(NDC)][-1],
                                 inc="mm")
                    if rms:
                        dk.update(bcol=2, gaincol=8,
                                  outap=Kall[:, oc, ts(i, TCH)])
                        dk["psS2w"] = qe_prev
                        dk["psTw"] = qe_prev
                        dk["psRw"] = qe_prev
                        stk = emit_rms(mmk, psk, ob, oc, tkk, sqq, dk)
                        dk["tdst_war"] = stk
                        dq["psS2w"] = stk
                        dq["psTw"] = stk
                        dq["psRw"] = stk
                        ke_prev = stk
                    else:
                        ke_prev = p.step(
                            "vector", [mmk, smtok],
                            lambda v, _i=i, _oc=oc, _ob=ob: v.tensor_scalar(
                                Kall[:, _oc, ts(_i, TCH)], psk[:, _ob, :],
                                sm[:, _oc, 2:3], None, ALU.add),
                            inc="ev", fence=True)
                kev_last[i] = ke_prev
                qw[i] = p.step("sync", [qe_prev],
                               lambda sy, _i=i: sy.dma_start(
                                   out=qd[:, :, ts(_i, TCH)],
                                   in_=qst[:, 0, :, :]),
                               inc="qs", amt=16)
            if "kall_dbg" in b.dram:
                p.step("gpsimd", [kev_last[NTC - 1]],
                       lambda g: g.dma_start(
                           out=b.dram["kall_dbg"].ap(),
                           in_=Kall[:, :, :].rearrange("p c t -> p (c t)")),
                       inc="qg", amt=16)
            p.step("sync", [qw[NTC - 1]], lambda sy: sy.nop())
            p.flush()
        b.barrier()

        if gbias:
          with contextlib.ExitStack() as x2:
            wfqb = x2.enter_context(nc.sbuf_tensor("gb_wfq", [128, NDC, D], BF16))
            qb = x2.enter_context(nc.sbuf_tensor("gb_qb", [128, NDC, TCH], BF16))
            gacc = x2.enter_context(nc.sbuf_tensor("gb_ga", [128, NDC], F32))
            gtmp = x2.enter_context(nc.sbuf_tensor("gb_gt", [128, NDC], F32))
            gaccb = x2.enter_context(nc.sbuf_tensor("gb_gb", [128, NDC], BF16))
            gvt = x2.enter_context(nc.sbuf_tensor("gb_gv", [128, NDC], F32))
            upt = x2.enter_context(nc.sbuf_tensor("gb_up", [128, NDC, 32], F32))
            uptb = x2.enter_context(nc.sbuf_tensor("gb_upb", [128, NDC, 32], BF16))
            ct = x2.enter_context(nc.sbuf_tensor("gb_ct", [128, TCH], F32))
            ctb = x2.enter_context(nc.sbuf_tensor("gb_ctb", [128, TCH], BF16))
            pt = x2.enter_context(nc.sbuf_tensor("gb_pt", [16, TCH], F32))
            zt = x2.enter_context(nc.sbuf_tensor("gb_zt", [16, TCH], F32))
            w2t = x2.enter_context(nc.sbuf_tensor("gb_w2", [16, TCH], F32))
            row = x2.enter_context(nc.sbuf_tensor("gb_row", [16, 3, TCH], F32))
            rowb = x2.enter_context(nc.sbuf_tensor("gb_rowb", [16, 3, TCH], BF16))
            abcd = b.dram[pre + "abcd"].ap()
            psG = x2.enter_context(nc.psum_tensor("gb_psG", [128, 8], F32))
            psC = x2.enter_context(nc.psum_tensor("gb_psC", [128, 512], F32))
            psP = x2.enter_context(nc.psum_tensor("gb_psP", [16, 512], F32))
            psZ = x2.enter_context(nc.psum_tensor("gb_psZ", [16, 512], F32))
            p = P(b)
            t1 = p.step("gpsimd", [],
                        lambda g: g.dma_start(
                            out=wfqb[:, :, :],
                            in_=b.dram[pre + "Wfq"].ap()
                            .rearrange("(c p) o -> p c o", p=128)),
                        inc="qg", amt=16)
            t2 = p.step("gpsimd", [],
                        lambda g: g.dma_start(out=upt[:, :, :],
                                              in_=b.dram[pre + "uptile"].ap()),
                        inc="qg", amt=16)
            t3 = p.step("gpsimd", [],
                        lambda g: g.dma_start(out=kar[:, :],
                                              in_=b.dram[pre + "karr"].ap()),
                        inc="qg", amt=16)
            last = p.step("vector", [t1, t2, t3],
                          lambda v: (v.tensor_copy(uptb[:, :, :],
                                                   upt[:, :, :]),
                                     v.drain())[-1], inc="ev")
            for i in range(NTC):
                ld = p.step("sync", [last],
                            lambda sy, _i=i: sy.dma_start(
                                out=qb[:, :, :], in_=qd[:, :, ts(_i, TCH)]),
                            inc="qs", amt=16)

                def gsum(v, _i=i):
                    v.tensor_reduce(gtmp[:, :], qb[:, :, :],
                                    mybir.AxisListType.X, ALU.add)
                    v.drain()
                    if _i == 0:
                        v.tensor_copy(gacc[:, :], gtmp[:, :])
                    else:
                        v.tensor_add(gacc[:, :], gacc[:, :], gtmp[:, :])
                    return v.drain()

                last = p.step("vector", [ld], gsum, inc="ev")
            last = p.step("vector", [last],
                          lambda v: (v.tensor_single_scalar(
                              gacc[:, :], gacc[:, :], 1.0 / T, ALU.mult),
                              v.drain(),
                              v.tensor_copy(gaccb[:, :], gacc[:, :]),
                              v.drain())[-1], inc="ev")
            wfgb = qb
            wfgt = x2.enter_context(nc.sbuf_tensor("gb_wfg", [128, NDC, D], BF16))
            t4 = p.step("gpsimd", [last],
                        lambda g: g.dma_start(
                            out=wfgt[:, :, :],
                            in_=b.dram[pre + "Wfg"].ap()
                            .rearrange("(c p) o -> p c o", p=128)),
                        inc="qg", amt=16)
            for oc in range(NDC):
                mm = p.step("tensor", [t4, last],
                            lambda t, _oc=oc: [
                                t.matmul(psG[:, _oc:_oc + 1],
                                         wfgt[:, ic, ts(_oc, 128)],
                                         gaccb[:, ic:ic + 1],
                                         start=(ic == 0),
                                         stop=(ic == NDC - 1))
                                for ic in range(NDC)][-1], inc="mm")
                last = mm
            def gvmath(v):
                v.tensor_copy(gvt[:, :], psG[:, :])
                v.drain()
                for dc in range(NDC):
                    v.tensor_scalar(gvt[:, dc:dc + 1], gvt[:, dc:dc + 1],
                                    sm[:, dc, 4:5], sm[:, dc, 3:4],
                                    ALU.add, ALU.add)
                return v.drain()

            last = p.step("vector", [last], gvmath, inc="ev")
            for i in range(NTC):
                ld = p.step("sync", [last],
                            lambda sy, _i=i: sy.dma_start(
                                out=qb[:, :, :], in_=qd[:, :, ts(_i, TCH)]),
                            inc="qs", amt=16)
                last = ld
                for oc in range(NDC):
                    mm = p.step("tensor", [last],
                                lambda t, _oc=oc: [
                                    t.matmul(psC[:, :],
                                             wfqb[:, ic, ts(_oc, 128)],
                                             qb[:, ic, :],
                                             start=(ic == 0),
                                             stop=(ic == NDC - 1))
                                    for ic in range(NDC)][-1], inc="mm")
                    sc = p.step("scalar", [mm],
                                lambda s, _oc=oc: s.activation(
                                    ct[:, :], psC[:, :], AF.Tanh,
                                    bias=gvt[:, _oc:_oc + 1]),
                                inc="es", fence=True)
                    cv = p.step("vector", [sc],
                                lambda v: (v.tensor_copy(ctb[:, :], ct[:, :]),
                                           v.drain())[-1], inc="ev")
                    mm2 = p.step("tensor", [cv],
                                 lambda t, _oc=oc: (
                                     t.matmul(psP[:, :], uptb[:, _oc, 0:16],
                                              ctb[:, :], start=(_oc == 0),
                                              stop=(_oc == NDC - 1)),
                                     t.matmul(psZ[:, :], uptb[:, _oc, 16:32],
                                              ctb[:, :], start=(_oc == 0),
                                              stop=(_oc == NDC - 1)))[-1],
                                 inc="mm")
                    last = mm2
                s1 = p.step("scalar", [last],
                            lambda s: (s.activation(pt[:, :], psP[:, :],
                                                    AF.Sigmoid),
                                       s.activation(zt[:, :], psZ[:, :],
                                                    AF.Sigmoid))[-1],
                            inc="es", fence=True)

                def pzmath(v, _i=i):
                    v.tensor_single_scalar(pt[:, :], pt[:, :], float(T),
                                           ALU.mult)
                    v.tensor_single_scalar(zt[:, :], zt[:, :], float(T),
                                           ALU.mult)
                    v.drain()
                    v.reciprocal(zt[:, :], zt[:, :])
                    v.drain()
                    v.tensor_mul(w2t[:, :], zt[:, :], zt[:, :])
                    v.drain()
                    v.tensor_single_scalar(w2t[:, :], w2t[:, :], 16.0,
                                           ALU.mult)
                    v.drain()
                    v.tensor_single_scalar(row[:, 0, :], w2t[:, :], -1.0,
                                           ALU.mult)
                    v.tensor_mul(row[:, 1, :], pt[:, :], w2t[:, :])
                    v.drain()
                    v.tensor_single_scalar(row[:, 1, :], row[:, 1, :], 2.0,
                                           ALU.mult)
                    v.tensor_mul(row[:, 2, :], pt[:, :], pt[:, :])
                    v.drain()
                    v.tensor_mul(row[:, 2, :], row[:, 2, :], w2t[:, :])
                    v.drain()
                    v.tensor_single_scalar(row[:, 2, :], row[:, 2, :], -1.0,
                                           ALU.mult)
                    return v.drain()

                last = p.step("vector", [s1], pzmath, inc="ev")

                last = p.step("vector", [last],
                              lambda v: (v.tensor_copy(rowb[:, :, :],
                                                       row[:, :, :]),
                                         v.drain())[-1], inc="ev")
                last = p.step("sync", [last],
                              lambda sy, _i=i: sy.dma_start(
                                  out=abcd[:, :, ts(_i, TCH)],
                                  in_=rowb[:, :, :]),
                              inc="qs", amt=16)
            p.step("vector", [last], lambda v: v.engine_nop())
            p.flush()
          b.barrier()

        Vtm = xs.enter_context(nc.sbuf_tensor("at_V", [128, 512 * 65], BF16))
        Vv = Vtm.rearrange("p (s c) -> p s c", c=65)
        with contextlib.ExitStack() as x2:
            wv = x2.enter_context(nc.sbuf_tensor("at_wv", [128, NDC, D], BF16))
            lnb = x2.enter_context(nc.sbuf_tensor("at_ln2", [128, 2, NDC, TCH], BF16))
            bvrep = x2.enter_context(nc.sbuf_tensor("at_bvr", [128, D], F32))
            psv = x2.enter_context(nc.psum_tensor("at_psv", [128, 2, 512], F32))
            psb2 = x2.enter_context(nc.psum_tensor("at_psb2", [128, 512], F32))
            p = P(b)
            wvt = p.step("gpsimd", [],
                         lambda g: g.dma_start(
                             out=wv[:, :, :],
                             in_=b.dram[pre + "Wv"].ap()
                             .rearrange("(c p) o -> p c o", p=128)),
                         inc="qg", amt=16)
            bvt = p.step("gpsimd", [],
                         lambda g: g.dma_start(
                             out=bvrep[:, :],
                             in_=b.dram[pre + "bvrow"].ap()),
                         inc="qg", amt=16)
            vone = p.step("vector", [],
                          lambda v: v.memset(Vv[:, :, 64:65], 1.0),
                          inc="ev")
            bvh = bvrep.rearrange("p (h d) -> p h d", d=64)
            ld = [None] * NTC
            vev = [None] * NTC
            for i in range(NTC):
                bi = i % 2
                ld[i] = p.step("sync",
                               [vev[i - 2] if i >= 2 else None],
                               lambda sy, _i=i, _bi=bi: sy.dma_start(
                                   out=lnb[:, _bi, :, :],
                                   in_=lnd[:, :, ts(_i, TCH)]),
                               inc="qs", amt=16)
                ve_prev = None
                for tt in range(4):
                    kt = i * 4 + tt
                    tb = tt % 2
                    mmv = p.step("tensor",
                                 [ld[i], wvt, ve_prev],
                                 lambda t, _bi=bi, _tt=tt, _tb=tb: [
                                     t.matmul(psv[:, _tb, :],
                                              lnb[:, _bi, ic, ts(_tt, 128)],
                                              wv[:, ic, 0:512],
                                              start=(ic == 0),
                                              stop=(ic == NDC - 1))
                                     for ic in range(NDC)][-1],
                                 inc="mm")
                    mmv2 = p.step("tensor", [ld[i], wvt, ve_prev],
                                  lambda t, _bi=bi, _tt=tt, _tb=tb: [
                                      t.matmul(psb2[:, :],
                                               lnb[:, _bi, ic, ts(_tt, 128)],
                                               wv[:, ic, 512:1024],
                                               start=(ic == 0),
                                               stop=(ic == NDC - 1))
                                      for ic in range(NDC)][-1],
                                  inc="mm")
                    def vevf(v, _kt=kt, _tb=tb):
                        v.tensor_tensor(
                            Vv[:, _kt::32, 0:64][:, 0:8, :],
                            psv[:, _tb, :].rearrange("p (h d) -> p h d", d=64),
                            bvh[:, 0:8, :], ALU.add)
                        return v.tensor_tensor(
                            Vv[:, _kt::32, 0:64][:, 8:16, :],
                            psb2[:, :].rearrange("p (h d) -> p h d", d=64),
                            bvh[:, 8:16, :], ALU.add)
                    ve_prev = p.step("vector", [mmv, mmv2, bvt], vevf,
                                     inc="ev", fence=True)
                vev[i] = ve_prev
            p.step("vector", [vev[NTC - 1]], lambda v: v.engine_nop())
            p.flush()
        b.barrier()

        with contextlib.ExitStack() as x2:
            wo = x2.enter_context(nc.sbuf_tensor("at_wo", [128, NDC, D], BF16))
            qch = x2.enter_context(nc.sbuf_tensor("at_qc", [128, 1, NDC, TCH], BF16))
            expS = x2.enter_context(nc.sbuf_tensor("at_eS", [128, 2, TCH], BF16))
            ctxc = x2.enter_context(nc.sbuf_tensor("at_cx", [128, NDC, TCH], BF16))
            gbt = x2.enter_context(nc.sbuf_tensor("at_gb", [128, TCH], BF16))
            if causal:
                expF = x2.enter_context(nc.sbuf_tensor("at_eF", [128, TCH], F32))
            if gate:
                ghr = x2.enter_context(nc.sbuf_tensor("at_ghr", [1, TCH], F32))
                sgr2 = x2.enter_context(nc.sbuf_tensor("at_sg2", [1, TCH], F32))
                gdram = b.dram[pre + "gdram"].ap()
            sgr = x2.enter_context(nc.sbuf_tensor("at_sg", [1, TCH], F32))
            if gbias:
                xb3 = x2.enter_context(nc.sbuf_tensor("at_xb", [128, 1, 2, TCH], F32))
                abc3 = x2.enter_context(nc.sbuf_tensor("at_abc3", [3, TCH], BF16))
            else:
                xb3 = x2.enter_context(nc.sbuf_tensor("at_xb", [128, 1, NDC, TCH], F32))
            psS = x2.enter_context(nc.psum_tensor("at_psS", [128, 2, 512], F32))
            psC = x2.enter_context(nc.psum_tensor("at_psC", [128, 512], F32))
            psB = x2.enter_context(nc.psum_tensor("at_psB", [128, 512], F32))
            psO = x2.enter_context(nc.psum_tensor("at_psO", [128, 2, 512], F32))
            p = P(b)
            wot = p.step("gpsimd", [],
                         lambda g: g.dma_start(
                             out=wo[:, :, :],
                             in_=b.dram[pre + "Wo"].ap()
                             .rearrange("(c p) o -> p c o", p=128)),
                         inc="qg", amt=16)
            ldq = [None] * NTC
            oev_last = [None] * NTC
            stw = [None] * NTC
            for i in range(NTC):
                bi = i % 2
                ldq[i] = p.step("sync",
                                [oev_last[i - 1] if i >= 1 else None],
                                lambda sy, _i=i: sy.dma_start(
                                    out=qch[:, 0, :, :],
                                    in_=qd[:, :, ts(_i, TCH)]),
                                inc="qs", amt=16)
                if gbias:
                    ldx = None
                else:
                    ldx = p.step("sync", [stw[i - 1] if i >= 1 else None],
                                 lambda sy, _i=i: sy.dma_start(
                                     out=xb3[:, 0, :, :],
                                     in_=resid[:, :, ts(_i, TCH)]),
                                 inc="qs", amt=16)
                kt_max = 32 if not causal else ((i + 1) * 4)
                cx_done = None
                gbe_prev = None
                mml_prev = oev_last[i - 1] if i >= 1 else None
                mulg_prev = None
                for h in range(NH):
                    hb = (h % 2) * 64
                    hc = h // 2
                    abccp = None
                    if gbias:
                        abcd_s = b.dram[pre + "abcd"].ap()
                        abccp = p.step("sync", [cx_done],
                                       lambda sy, _h=h, _i=i: sy.dma_start(
                                           out=abc3[:, :],
                                           in_=abcd_s[_h, :, ts(_i, TCH)]),
                                       inc="qs", amt=16)
                    if gate:
                        ldg = p.step("sync", [mulg_prev],
                                     lambda sy, _i=i, _h=h: sy.dma_start(
                                         out=ghr[:, :],
                                         in_=gdram[_h:_h + 1, ts(_i, TCH)]),
                                     inc="qs", amt=16)
                    ex_prev = [None, None]
                    cm_prev = None
                    for kt in range(kt_max):
                        sb = kt % 2
                        def _smm(t, _bi=bi, _h=h, _hb=hb, _hc=hc, _kt=kt,
                                 _sb=sb, _i=i):
                            if gbias:
                                t.matmul(psS[:, _sb, :],
                                         kar[:, ts(_kt, 128)],
                                         abc3[:, :],
                                         start=True, stop=False)
                            return t.matmul(
                                psS[:, _sb, :],
                                Kall[_hb:_hb + 64, _hc, ts(_kt, 128)],
                                qch[_hb:_hb + 64, 0, _hc, :],
                                start=(not gbias), stop=True)

                        mms = p.step(
                            "tensor",
                            [ldq[i], ex_prev[sb],
                             cx_done if kt == 0 else None,
                             abccp if kt == 0 else None,
                             mml_prev if (kt == 0 and h == 0) else None],
                            _smm, inc="mm")
                        band = causal and kt >= i * 4
                        exk = p.step(
                            "scalar", [mms, cm_prev if band else None],
                            lambda s, _sb=sb, _band=band: s.activation(
                                (expF[:, :] if _band else expS[:, _sb, :]),
                                psS[:, _sb, :], AF.Exp,
                                scale=float(scale)),
                            inc="es")
                        if band:
                            off = (kt - i * 4) * 128

                            def cmask(v, _sb=sb, _off=off):
                                v.tensor_mul(expS[:, _sb, _off:_off + 128],
                                             expF[:, _off:_off + 128],
                                             b.consts["mask128f"][:, :])
                                if _off > 0:
                                    v.memset(expS[:, _sb, 0:_off], 0.0)
                                if _off + 128 < TCH:
                                    v.tensor_copy(expS[:, _sb, _off + 128:],
                                                  expF[:, _off + 128:])
                                return v.drain()

                            exk = p.step("vector", [exk], cmask, inc="ev")
                            cm_prev = exk
                            if ("at_dbg" in b.dram and i == 0 and h == 0
                                    and kt == 0):
                                dbg = b.dram["at_dbg"].ap()
                                et = p.step("vector", [exk],
                                            lambda v: v.tensor_copy(
                                                ctxc[:, 0:1, 0:512][:, 0, :],
                                                expS[:, 0, :]),
                                            inc="ev", fence=True)
                                p.step("gpsimd", [et],
                                       lambda g: g.dma_start(
                                           out=dbg[:, 0:512],
                                           in_=ctxc[:, 0, :]),
                                       inc="qg", amt=16)
                                mt = p.step("vector", [exk],
                                            lambda v: v.tensor_copy(
                                                msk[:, :],
                                                b.consts["mask128"][:, :]),
                                            inc="ev", fence=True)
                                p.step("gpsimd", [mt],
                                       lambda g: g.dma_start(
                                           out=dbg[:, 512:640],
                                           in_=msk[:, :]),
                                       inc="qg", amt=16)
                        mmc = p.step(
                            "tensor", [exk],
                            lambda t, _h=h, _kt=kt, _sb=sb,
                            _st=(kt == 0), _sp=(kt == kt_max - 1): t.matmul(
                                psC[0:65, :], Vv[:, _h * 32 + _kt, 0:65],
                                expS[:, _sb, :],
                                start=_st, stop=_sp),
                            inc="mm")
                        ex_prev[sb] = mmc
                    rc = p.step("vector", [ex_prev[(kt_max - 1) % 2]],
                                lambda v: v.reciprocal(sgr[:, :],
                                                       psC[64:65, :]),
                                inc="ev", fence=True)
                    if "at_dbg" in b.dram and i == 0 and h == 0:
                        ce = p.step("vector", [rc],
                                    lambda v: v.tensor_copy(
                                        xb3[:, 0, 0:1, :][:, 0, :],
                                        psC[:, :]),
                                    inc="ev", fence=True)
                        p.step("sync", [ce],
                               lambda sy: sy.dma_start(
                                   out=b.dram["at_dbg"].ap()[:, 640:1152],
                                   in_=xb3[:, 0, 0, :]),
                               inc="qs", amt=16)
                    if gate:
                        rc = p.step("vector", [rc, ldg],
                                    lambda v: v.tensor_mul(
                                        sgr2[:, :], sgr[:, :], ghr[:, :]),
                                    inc="ev", fence=True)
                        mulg_prev = rc
                    sgr_use = sgr2 if gate else sgr
                    mmb = p.step("tensor", [rc, gbe_prev],
                                 lambda t, _sg=sgr_use: t.matmul(
                                     psB[:, :], b.consts["ones_rowf"][:, :],
                                     _sg[:, :], start=True, stop=True),
                                 inc="mm")
                    gbe = p.step("vector", [mmb],
                                 lambda v: v.tensor_copy(gbt[:, :],
                                                         psB[:, :]),
                                 inc="ev")
                    gbe_prev = gbe
                    cx_done = p.step(
                        "vector", [gbe],
                        lambda v, _h=h, _hb=hb, _hc=hc: v.tensor_mul(
                            ctxc[_hb:_hb + 64, _hc, :], psC[0:64, :],
                            gbt[_hb:_hb + 64, :]),
                        inc="ev", fence=(h == NH - 1))
                ctok = None
                if "ctx_dbg" in b.dram and i == 0:
                    ctok = p.step("gpsimd", [cx_done],
                                  lambda g: g.dma_start(
                                      out=b.dram["ctx_dbg"].ap(),
                                      in_=ctxc[:, :, :].rearrange(
                                          "p c t -> p (c t)")),
                                  inc="qg", amt=16)
                oe_prev = None
                stob = [stw[i - 1] if i >= 1 else None] * 2
                for oc in range(NDC):
                    ob = oc % 2
                    mmo = p.step("tensor",
                                 [cx_done, wot, oe_prev,
                                  ctok if oc == 0 else None],
                                 lambda t, _oc=oc, _ob=ob: [
                                     t.matmul(psO[:, _ob, :],
                                              wo[:, hp, ts(_oc, 128)],
                                              ctxc[:, hp, :],
                                              start=(hp == 0),
                                              stop=(hp == NDC - 1))
                                     for hp in range(NDC)][-1],
                                 inc="mm")
                    if gbias:
                        lxo = p.step("sync", [stob[ob]],
                                     lambda sy, _i=i, _oc=oc, _ob=ob:
                                     sy.dma_start(
                                         out=xb3[:, 0, _ob, :],
                                         in_=resid[:, _oc, ts(_i, TCH)]),
                                     inc="qs", amt=16)
                        oe_prev = p.step(
                            "vector", [mmo, lxo, smtok],
                            lambda v, _oc=oc, _ob=ob:
                            v.scalar_tensor_tensor(
                                xb3[:, 0, _ob, :], psO[:, _ob, :],
                                sm[:, _oc, 0:1], xb3[:, 0, _ob, :],
                                ALU.add, ALU.add),
                            inc="ev")
                        stob[ob] = p.step("sync", [oe_prev],
                                          lambda sy, _i=i, _oc=oc, _ob=ob:
                                          sy.dma_start(
                                              out=resid[:, _oc, ts(_i, TCH)],
                                              in_=xb3[:, 0, _ob, :]),
                                          inc="qs", amt=16)
                    else:
                        oe_prev = p.step(
                            "vector", [mmo, ldx, smtok],
                            lambda v, _oc=oc, _ob=ob:
                            v.scalar_tensor_tensor(
                                xb3[:, 0, _oc, :], psO[:, _ob, :],
                                sm[:, _oc, 0:1], xb3[:, 0, _oc, :],
                                ALU.add, ALU.add),
                            inc="ev")
                oev_last[i] = oe_prev
                if gbias:
                    stw[i] = stob[1]
                else:
                    stw[i] = p.step("sync", [oe_prev],
                                    lambda sy, _i=i: sy.dma_start(
                                        out=resid[:, :, ts(_i, TCH)],
                                        in_=xb3[:, 0, :, :]),
                                    inc="qs", amt=16)
            p.step("sync", [stw[NTC - 1]], lambda sy: sy.nop())
            p.flush()
    b.barrier()


def stage_attn_std(pre, w, names=("Wq", "Wk", "Wv", "Wo"),
                   bo=None, bq=None, bk=None, bv=None, transpose=False):
    out = {}
    for nm in names:
        mat = w[nm]
        if transpose:
            mat = mat.T
        out[pre + nm] = np.ascontiguousarray(mat)
    smalls = np.zeros((128, NDC, 128), np.float32)
    for j, vv in enumerate([bo, bq, bk, bv]):
        if vv is not None:
            smalls[:, :, j] = col_of(w[vv])[:, :, 0]
    out[pre + "smalls"] = smalls
    bv_vec = np.zeros(D, np.float32) if bv is None else w[bv]
    out[pre + "bvrow"] = np.tile(bv_vec[None, :], (128, 1))
    return out


def stage_attn_l1a(pre, w):
    Wqfull = w["Wq"]
    bqfull = w["bq"]
    out = {
        pre + "Wq": np.ascontiguousarray(Wqfull[:, 0:D]),
        pre + "Wk": np.ascontiguousarray(w["Wk"]),
        pre + "Wv": np.ascontiguousarray(w["Wv"]),
        pre + "Wo": np.ascontiguousarray(w["Wo"]),
    }
    wg = np.zeros((D, 128), np.float32)
    wg[:, 0:16] = Wqfull[:, D:D + 16]
    out[pre + "Wgpad"] = wg
    smalls = np.zeros((128, NDC, 128), np.float32)
    smalls[:, :, 0] = col_of(w["bo"])[:, :, 0]
    smalls[:, :, 1] = col_of(bqfull[0:D])[:, :, 0]
    smalls[:, :, 2] = col_of(w["bk"])[:, :, 0]
    smalls[0:16, 0, 6] = bqfull[D:D + 16]
    for p_ in range(128):
        smalls[p_, :, 7] = w["qn_w"][p_ % 64]
        smalls[p_, :, 8] = w["kn_w"][p_ % 64]
    out[pre + "smalls"] = smalls
    out[pre + "bvrow"] = np.tile(w["bv"][None, :], (128, 1))
    return out


def attn_l1a_dram_specs(pre):
    sp = attn_dram_specs(pre)
    sp[pre + "Wgpad"] = ([D, 128], F32)
    return sp


def rope_tables(rot_dim, interleaved_repeat):
    theta = 10000.0
    tpos = np.arange(T, dtype=np.float64)
    cosb = np.zeros((128, T), np.float32)
    sinb = np.zeros((128, T), np.float32)
    for p in range(128):
        d = p % 64
        if d >= rot_dim:
            cosb[p, :] = 1.0
            sinb[p, :] = 0.0
            continue
        i = d // 2
        f = theta ** (-2.0 * i / rot_dim)
        cosb[p, :] = np.cos(tpos * f)
        sgn = -1.0 if d % 2 == 0 else 1.0
        sinb[p, :] = sgn * np.sin(tpos * f)
    return cosb, sinb


def stage_attn_rope(pre, w):
    wa = w["w_attn"]
    ba = w["b_attn"]
    Wq, Wk, Wv = wa[:, 0:D], wa[:, D:2 * D], wa[:, 2 * D:3 * D]
    bq, bk, bv = ba[0:D], ba[D:2 * D], ba[2 * D:3 * D]
    perm = np.arange(D) ^ 1
    out = {
        pre + "Wq": np.ascontiguousarray(Wq),
        pre + "Wk": np.ascontiguousarray(Wk),
        pre + "Wv": np.ascontiguousarray(Wv),
        pre + "Wo": np.ascontiguousarray(w["w_proj"]),
        pre + "Wqsw": np.ascontiguousarray(Wq[:, perm]),
        pre + "Wksw": np.ascontiguousarray(Wk[:, perm]),
    }
    smalls = np.zeros((128, NDC, 128), np.float32)
    smalls[:, :, 0] = col_of(w["b_proj"])[:, :, 0]
    smalls[:, :, 1] = col_of(bq)[:, :, 0]
    smalls[:, :, 2] = col_of(bk)[:, :, 0]
    smalls[:, :, 4] = col_of(bq[perm])[:, :, 0]
    smalls[:, :, 5] = col_of(bk[perm])[:, :, 0]
    out[pre + "smalls"] = smalls
    out[pre + "bvrow"] = np.tile(bv[None, :], (128, 1))
    cosb, sinb = rope_tables(64, True)
    out[pre + "ropec"] = cosb
    out[pre + "ropes"] = sinb
    return out


def attn_rope_dram_specs(pre):
    sp = attn_dram_specs(pre)
    sp[pre + "Wqsw"] = ([D, D], F32)
    sp[pre + "Wksw"] = ([D, D], F32)
    sp[pre + "ropec"] = ([128, T], F32)
    sp[pre + "ropes"] = ([128, T], F32)
    return sp


def attn_dram_specs(pre):
    return {
        pre + "Wq": ([D, D], F32), pre + "Wk": ([D, D], F32),
        pre + "Wv": ([D, D], F32), pre + "Wo": ([D, D], F32),
        pre + "smalls": ([128, NDC, 128], F32),
        pre + "bvrow": ([128, D], F32),
    }


def build_program(sublayers, dram_specs, with_consts=True):
    nc = bass.Bass("TRN2", target_bir_lowering=False)
    dram = {}
    for name, (shape, dt, kind) in dram_specs.items():
        if kind == "in":
            dram[name] = nc.dram_tensor(name, shape, dt, kind="ExternalInput")
        elif kind == "out":
            dram[name] = nc.dram_tensor(name, shape, dt, kind="ExternalOutput")
        elif kind == "shared":
            dram[name] = nc.dram_tensor(name, shape, dt, addr_space="Shared")
        else:
            dram[name] = nc.dram_tensor(name, shape, dt)

    import contextlib
    with contextlib.ExitStack() as stack:
        sems = {}
        for s in SEMS:
            sems[s] = stack.enter_context(nc.semaphore(s))
        consts = {}
        if with_consts:
            for cname, (shape, dt) in CONST_SPECS.items():
                consts[cname] = stack.enter_context(
                    nc.sbuf_tensor("c_" + cname, shape, dt))
        block = stack.enter_context(nc.Block())
        b = Bld(nc, block, sems, consts, dram)

        if with_consts:
            import contextlib as _ctl
            cstack = _ctl.ExitStack()
            pconst = P(b)
            pconst.step("vector", [],
                        lambda v: v.memset(consts["ones_col"][:, :], 1.0),
                        inc="ev")
            pconst.step("vector", [],
                        lambda v: v.memset(consts["ones_row"][:, :], 1.0),
                        inc="ev")

            for cname in CONST_SPECS:
                if cname in ("ones_col", "ones_row"):
                    continue
                shape, dt = CONST_SPECS[cname]
                if dt == F32:
                    pconst.step("gpsimd", [],
                                lambda g, _c=cname: g.dma_start(
                                    out=consts[_c][:, :],
                                    in_=b.dram["cst_" + _c].ap()),
                                inc="qg", amt=16)
                else:
                    fstage = cstack.enter_context(
                        nc.sbuf_tensor("cf_" + cname, shape, F32))
                    t1 = pconst.step("gpsimd", [],
                                     lambda g, _c=cname, _f=fstage: g.dma_start(
                                         out=_f[:, :],
                                         in_=b.dram["cst_" + _c].ap()),
                                     inc="qg", amt=16)
                    pconst.step("vector", [t1],
                                lambda v, _c=cname, _f=fstage: v.tensor_copy(
                                    consts[_c][:, :], _f[:, :]),
                                inc="ev")
            sinkt = cstack.enter_context(
                nc.sbuf_tensor("cst_sink_t", [1, 64], F32))

            def sinkf(v):
                for ci, cn in enumerate(CONST_SPECS):
                    v.tensor_copy(sinkt[0:1, ci:ci + 1],
                                  consts[cn][0:1, 0:1])
                return v.drain()

            stok = pconst.step("vector", [], sinkf, inc="ev")
            pconst.step("sync", [stok],
                        lambda sy: sy.dma_start(
                            out=b.dram["cst_sink"].ap(), in_=sinkt[:, :]),
                        inc="qs", amt=16)
            pconst.flush()
            b.barrier()
            cstack.close()

        _orig_sb = nc.sbuf_tensor
        _orig_ps = nc.psum_tensor
        _uid = [0]

        def _sb(name, *a, **k):
            return _orig_sb(f"{name}_u{_uid[0]}", *a, **k)

        def _ps(name, *a, **k):
            return _orig_ps(f"{name}_u{_uid[0]}", *a, **k)

        nc.sbuf_tensor = _sb
        nc.psum_tensor = _ps
        for fn, kw in sublayers:
            _uid[0] += 1
            fn(b, **kw)
        nc.sbuf_tensor = _orig_sb
        nc.psum_tensor = _orig_ps

    return nc, b


def base_dram_specs(dev_io=True):
    sp = {}
    for cname, (shape, dt) in CONST_SPECS.items():
        sp["cst_" + cname] = (shape, dt, "in")
    sp["cst_sink"] = ([1, 64], F32, "internal")
    sp["resid"] = ([D, T], F32, "internal")
    sp["ln_dram"] = ([D, T], BF16, "internal")
    sp["xn_dram"] = ([D, T], BF16, "internal")
    sp["h_dram"] = ([32768, T], BF16, "internal")
    sp["oacc_dram"] = ([D, T], F32, "internal")
    sp["q_dram"] = ([D, T], BF16, "internal")
    if dev_io:
        sp["x_in"] = ([D, T], F32, "in")
        sp["x_out"] = ([D, T], F32, "out")
    return sp


def emit_load_resid(b):
    p = P(b)
    tok = p.step("gpsimd", [],
                 lambda g: g.dma_start(out=b.dram["resid"].ap(),
                                       in_=b.dram["x_in"].ap()),
                 inc="qg", amt=16)
    p.step("gpsimd", [tok], lambda g: g.nop())
    p.flush()
    b.barrier()


def emit_store_resid(b):
    p = P(b)
    tok = p.step("gpsimd", [],
                 lambda g: g.dma_start(out=b.dram["x_out"].ap(),
                                       in_=b.dram["resid"].ap()),
                 inc="qg", amt=16)
    p.step("gpsimd", [tok], lambda g: g.nop())
    p.flush()
    b.barrier()


def host_const_inputs():
    hc = host_consts()
    return {"cst_" + k: np.ascontiguousarray(v) for k, v in hc.items()}


def kernel(**inputs):
    raise NotImplementedError("under construction")


def emit_ffn_l1(b, pre):
    nc = b.nc
    resid = b.dram["resid"].ap().rearrange("(c p) t -> p c t", p=128)
    lnd = b.dram["ln_dram"].ap().rearrange("(c p) t -> p c t", p=128)
    oacc = b.dram["oacc_dram"].ap().rearrange("(c p) t -> p c t", p=128)
    NE = 8
    TCH = 512
    NTC = T // TCH

    with contextlib.ExitStack() as xs:
        afm = xs.enter_context(nc.sbuf_tensor("f1_afm", [8, T], F32))
        last = None

        with contextlib.ExitStack() as x2:
            wrsf = x2.enter_context(nc.sbuf_tensor("f1_wrf", [128, NDC, 128], F32))
            wrs = x2.enter_context(nc.sbuf_tensor("f1_wr", [128, NDC, 128], BF16))
            gbt = x2.enter_context(nc.sbuf_tensor("f1_gb", [128, 32, 8], F32))
            xnb = x2.enter_context(nc.sbuf_tensor("f1_xn", [128, NDC, TCH], BF16))
            lgr = x2.enter_context(nc.sbuf_tensor("f1_lgr", [8, T], F32))
            lgtm = x2.enter_context(nc.sbuf_tensor("f1_tm", [128, 32, 8], F32))
            stm = x2.enter_context(nc.sbuf_tensor("f1_stm", [128, 32, 8], F32))
            sfc = x2.enter_context(nc.sbuf_tensor("f1_sfc", [128, 32, 8], F32))
            msk = x2.enter_context(nc.sbuf_tensor("f1_msk", [128, 32, 8], F32))
            tmp8 = x2.enter_context(nc.sbuf_tensor("f1_t8", [128, 32, 8], F32))
            atm = x2.enter_context(nc.sbuf_tensor("f1_atm", [128, 32, 8], F32))
            m1a = x2.enter_context(nc.sbuf_tensor("f1_m1", [128, 32], F32))
            m2a = x2.enter_context(nc.sbuf_tensor("f1_m2", [128, 32], F32))
            gs0 = x2.enter_context(nc.sbuf_tensor("f1_g0", [128, 32], F32))
            gs1 = x2.enter_context(nc.sbuf_tensor("f1_g1", [128, 32], F32))
            win1 = x2.enter_context(nc.sbuf_tensor("f1_w1", [128, 32], F32))
            win0 = x2.enter_context(nc.sbuf_tensor("f1_w0", [128, 32], F32))
            psL = x2.enter_context(nc.psum_tensor("f1_psL", [8, 512], F32))
            psT = x2.enter_context(nc.psum_tensor("f1_psT", [128, 512], F32))
            ptb = x2.enter_context(nc.psum_tensor("f1_ptb", [8, 128], F32))
            p = P(b)
            t0 = p.step("gpsimd", [],
                        lambda g: g.dma_start(
                            out=wrsf[:, :, :],
                            in_=b.dram[pre + "Wrpad"].ap()
                            .rearrange("(c p) e -> p c e", p=128)),
                        inc="qg", amt=16)
            t1 = p.step("gpsimd", [],
                        lambda g: g.dma_start(out=gbt[:, :, :],
                                              in_=b.dram[pre + "gbias"].ap()),
                        inc="qg", amt=16)
            last = p.step("vector", [t0, t1],
                          lambda v: (v.tensor_copy(wrs[:, :, :], wrsf[:, :, :]),
                                     v.drain())[-1], inc="ev")
            for i in range(NTC):
                ld = p.step("sync", [last],
                            lambda sy, _i=i: sy.dma_start(
                                out=xnb[:, :, :], in_=lnd[:, :, ts(_i, TCH)]),
                            inc="qs", amt=16)
                mm = p.step("tensor", [ld],
                            lambda t: [t.matmul(psL[:, :], wrs[:, ic, 0:8],
                                                xnb[:, ic, :],
                                                start=(ic == 0),
                                                stop=(ic == NDC - 1))
                                       for ic in range(NDC)][-1], inc="mm")
                last = p.step("vector", [mm],
                              lambda v, _i=i: (v.tensor_copy(
                                  lgr[:, ts(_i, TCH)], psL[:, :]),
                                  v.drain())[-1], inc="ev")
            for j in range(32):
                tr = p.step("tensor", [last],
                            lambda t, _j=j: t.transpose(
                                psT[:, (_j % 64) * 8:(_j % 64) * 8 + 8],
                                lgr[:, ts(_j, 128)],
                                b.consts["ident_f32"][0:8, 0:8]), inc="mm")
                if j == 31:
                    last = tr
            last = p.step("vector", [last],
                          lambda v: (v.tensor_copy(lgtm[:, :, :], psT[:, 0:256]),
                                     v.drain())[-1], inc="ev")
            last = p.step("scalar", [last],
                          lambda s: s.activation(stm[:, :, :], lgtm[:, :, :],
                                                 AF.Sigmoid),
                          inc="es", fence=True)

            def rmath(v):
                v.tensor_tensor(sfc[:, :, :], stm[:, :, :],
                                gbt[:, :].broadcast_to([128, 32, 8]), ALU.add)
                v.drain()
                for g, gsl in ((0, gs0), (1, gs1)):
                    sl = sfc[:, :, g * 4:g * 4 + 4]
                    v.tensor_reduce(m1a[:, :], sl, mybir.AxisListType.X,
                                    ALU.max)
                    v.drain()
                    v.tensor_tensor(tmp8[:, :, 0:4], sl,
                                    m1a[:, :].broadcast_to([128, 32, 4]),
                                    ALU.is_equal)
                    v.drain()
                    v.tensor_mul(tmp8[:, :, 0:4], tmp8[:, :, 0:4], sl)
                    v.drain()
                    v.tensor_sub(tmp8[:, :, 0:4], sl, tmp8[:, :, 0:4])
                    v.drain()
                    v.tensor_reduce(m2a[:, :], tmp8[:, :, 0:4],
                                    mybir.AxisListType.X, ALU.max)
                    v.drain()
                    v.tensor_add(gsl[:, :], m1a[:, :], m2a[:, :])
                    v.drain()
                v.tensor_tensor(win1[:, :], gs1[:, :], gs0[:, :], ALU.is_gt)
                v.drain()
                v.tensor_scalar(win0[:, :], win1[:, :], -1.0, 1.0,
                                ALU.mult, ALU.add)
                v.drain()
                v.tensor_tensor(msk[:, :, 0:4], sfc[:, :, 0:4],
                                win0[:, :].broadcast_to([128, 32, 4]),
                                ALU.mult)
                v.tensor_tensor(msk[:, :, 4:8], sfc[:, :, 4:8],
                                win1[:, :].broadcast_to([128, 32, 4]),
                                ALU.mult)
                v.drain()
                v.tensor_reduce(m1a[:, :], msk[:, :, :], mybir.AxisListType.X,
                                ALU.max)
                v.drain()
                v.tensor_tensor(tmp8[:, :, :], msk[:, :, :],
                                m1a[:, :].broadcast_to([128, 32, 8]),
                                ALU.is_equal)
                v.drain()
                v.tensor_mul(atm[:, :, :], tmp8[:, :, :], msk[:, :, :])
                v.drain()
                v.tensor_sub(atm[:, :, :], msk[:, :, :], atm[:, :, :])
                v.drain()
                v.tensor_reduce(m2a[:, :], atm[:, :, :], mybir.AxisListType.X,
                                ALU.max)
                v.drain()
                v.tensor_tensor(atm[:, :, :], atm[:, :, :],
                                m2a[:, :].broadcast_to([128, 32, 8]),
                                ALU.is_equal)
                v.drain()
                v.tensor_add(tmp8[:, :, :], tmp8[:, :, :], atm[:, :, :])
                v.drain()
                v.tensor_mul(tmp8[:, :, :], tmp8[:, :, :], stm[:, :, :])
                v.drain()
                v.tensor_reduce(m1a[:, :], tmp8[:, :, :], mybir.AxisListType.X,
                                ALU.add)
                v.drain()
                v.tensor_single_scalar(m1a[:, :], m1a[:, :], 1e-20, ALU.add)
                v.drain()
                v.reciprocal(m1a[:, :], m1a[:, :])
                v.drain()
                v.tensor_tensor(atm[:, :, :], tmp8[:, :, :],
                                m1a[:, :].broadcast_to([128, 32, 8]), ALU.mult)
                v.drain()
                v.tensor_single_scalar(atm[:, :, :], atm[:, :, :], 2.5,
                                       ALU.mult)
                return v.drain()

            last = p.step("vector", [last], rmath, inc="ev")
            for j in range(32):
                tr = p.step("tensor", [last],
                            lambda t, _j=j: t.transpose(
                                ptb[:, :], atm[:, _j, :],
                                b.consts["ident_f32"][:, :]), inc="mm")
                last = p.step("vector", [tr],
                              lambda v, _j=j: (v.tensor_copy(
                                  afm[:, ts(_j, 128)], ptb[:, :]),
                                  v.drain())[-1], inc="ev")
            p.step("vector", [last], lambda v: v.engine_nop())
            p.flush()
        b.barrier()

        hd = b.dram["h_dram"].ap()[0:48 * 128, :].rearrange(
            "(g p) t -> g p t", p=128)
        with contextlib.ExitStack() as x2:
            xnr = x2.enter_context(nc.sbuf_tensor("f1a_xn", [128, NDC, T], BF16))
            w1b = x2.enter_context(nc.sbuf_tensor("f1a_w1", [128, NDC, 16 * 128], BF16))
            abce = x2.enter_context(nc.sbuf_tensor("f1a_ab", [128, NTC, TCH], F32))
            ht = x2.enter_context(nc.sbuf_tensor("f1a_ht", [128, TCH], F32))
            ht2 = x2.enter_context(nc.sbuf_tensor("f1a_ht2", [128, TCH], F32))
            hstage = x2.enter_context(nc.sbuf_tensor("f1a_hs", [128, TCH], BF16))
            psA = x2.enter_context(nc.psum_tensor("f1a_psA", [128, 512], F32))
            psB = x2.enter_context(nc.psum_tensor("f1a_psB", [128, 512], F32))
            p = P(b)
            last = p.step("gpsimd", [],
                          lambda g: g.dma_start(out=xnr[:, :, :],
                                                in_=lnd[:, :, :]),
                          inc="qg", amt=16)
            for e in range(9):
                NF = 4 if e < 8 else 16
                if e < 8:
                    w1d = b.dram[pre + "W1"].ap()[e].rearrange(
                        "(c p) f -> p c f", p=128)
                    gbase = e * 4
                else:
                    w1d = b.dram[pre + "Ws1"].ap().rearrange(
                        "(c p) f -> p c f", p=128)
                    gbase = 32
                last = p.step("gpsimd", [last],
                              lambda g, _w=w1d, _nf=NF: g.dma_start(
                                  out=w1b[:, :, 0:_nf * 128], in_=_w),
                              inc="qg", amt=16)
                if e < 8:
                    for i in range(NTC):
                        mab = p.step("tensor", [last],
                                     lambda t, _i=i, _e=e: t.matmul(
                                         psB[:, :],
                                         b.consts["sel8"][:, ts(_e, 128)],
                                         afm[:, ts(_i, TCH)], start=True,
                                         stop=True), inc="mm")
                        last = p.step("vector", [mab],
                                      lambda v, _i=i: (v.tensor_copy(
                                          abce[:, _i, :], psB[:, :]),
                                          v.drain())[-1], inc="ev")
                for fc in range(NF):
                    for i in range(NTC):
                        mm1 = p.step("tensor", [last],
                                     lambda t, _fc=fc, _i=i: [
                                         t.matmul(psA[:, :],
                                                  w1b[:, ic, ts(_fc, 128)],
                                                  xnr[:, ic, ts(_i, TCH)],
                                                  start=(ic == 0),
                                                  stop=(ic == NDC - 1))
                                         for ic in range(NDC)][-1], inc="mm")
                        h1 = p.step("scalar", [mm1],
                                    lambda s: s.activation(ht[:, :],
                                                           psA[:, :],
                                                           AF.Relu),
                                    inc="es", fence=True)

                        def sqa(v, _e=e, _i=i):
                            v.tensor_mul(ht2[:, :], ht[:, :], ht[:, :])
                            v.drain()
                            if _e < 8:
                                v.tensor_mul(ht2[:, :], ht2[:, :],
                                             abce[:, _i, :])
                                v.drain()
                            v.tensor_copy(hstage[:, :], ht2[:, :])
                            return v.drain()

                        last = p.step("vector", [h1], sqa, inc="ev")
                        last = p.step("sync", [last],
                                      lambda sy, _g=gbase + fc, _i=i:
                                      sy.dma_start(
                                          out=hd[_g, :, ts(_i, TCH)],
                                          in_=hstage[:, :]),
                                      inc="qs", amt=16)
            p.step("sync", [last], lambda sy: sy.nop())
            p.flush()
        b.barrier()

        with contextlib.ExitStack() as x2:
            w2r = x2.enter_context(nc.sbuf_tensor("f1b_w2", [128, 16, D], BF16))
            hbuf = x2.enter_context(nc.sbuf_tensor("f1b_h", [128, 16, TCH], BF16))
            obuf = x2.enter_context(nc.sbuf_tensor("f1b_ob", [128, TCH], F32))
            psO = x2.enter_context(nc.psum_tensor("f1b_psO", [128, 512], F32))
            p = P(b)
            last = None
            for e in range(9):
                NF = 4 if e < 8 else 16
                if e < 8:
                    w2d = b.dram[pre + "W2"].ap()[e].rearrange(
                        "(c p) d -> p c d", p=128)
                    gbase = e * 4
                else:
                    w2d = b.dram[pre + "Ws2"].ap().rearrange(
                        "(c p) d -> p c d", p=128)
                    gbase = 32
                last = p.step("gpsimd", [last],
                              lambda g, _w=w2d, _nf=NF: g.dma_start(
                                  out=w2r[:, 0:_nf, :], in_=_w),
                              inc="qg", amt=16)
                for i in range(NTC):
                    def ldh(sy, _e=e, _i=i, _nf=NF, _g=gbase):
                        ins = None
                        for fc in range(_nf):
                            ins = sy.dma_start(
                                out=hbuf[:, fc, :],
                                in_=hd[_g + fc, :, ts(_i, TCH)])
                            ins.then_inc(b.h["qs"], 16)
                        return ins
                    last = p.step("sync", [last], ldh, inc="qs",
                                  amt=16 * NF, noinst=True)
                    for dc in range(NDC):
                        mm2 = p.step("tensor", [last],
                                     lambda t, _dc=dc, _nf=NF: [
                                         t.matmul(psO[:, :],
                                                  w2r[:, fc, ts(_dc, 128)],
                                                  hbuf[:, fc, :],
                                                  start=(fc == 0),
                                                  stop=(fc == _nf - 1))
                                         for fc in range(_nf)][-1], inc="mm")
                        oev = p.step("vector", [mm2],
                                     lambda v: (v.tensor_copy(obuf[:, :],
                                                              psO[:, :]),
                                                v.drain())[-1], inc="ev")
                        last = p.step("gpsimd", [oev],
                                      lambda g, _e=e, _i=i, _dc=dc:
                                      g.dma_start(
                                          out=oacc[:, _dc, ts(_i, TCH)],
                                          in_=obuf[:, :],
                                          **({} if _e == 0 else
                                             {"accum_op": ALU.add})),
                                      inc="qg", amt=16)
            p.step("gpsimd", [last], lambda g: g.nop())
            p.flush()
        b.barrier()

        with contextlib.ExitStack() as x2:
            oab = x2.enter_context(nc.sbuf_tensor("f1c_oa", [128, NDC, TCH], F32))
            xb2 = x2.enter_context(nc.sbuf_tensor("f1c_x", [128, NDC, TCH], F32))
            p = P(b)
            last = None
            for i in range(NTC):
                lo = p.step("sync", [last],
                            lambda sy, _i=i: sy.dma_start(
                                out=oab[:, :, :], in_=oacc[:, :, ts(_i, TCH)]),
                            inc="qs", amt=16)
                lx = p.step("sync", [lo],
                            lambda sy, _i=i: sy.dma_start(
                                out=xb2[:, :, :], in_=resid[:, :, ts(_i, TCH)]),
                            inc="qs", amt=16)
                ad = p.step("vector", [lx],
                            lambda v: ([v.tensor_add(oab[:, dc, :],
                                                     oab[:, dc, :],
                                                     xb2[:, dc, :])
                                        for dc in range(NDC)],
                                       v.drain())[-1], inc="ev")
                last = p.step("sync", [ad],
                              lambda sy, _i=i: sy.dma_start(
                                  out=resid[:, :, ts(_i, TCH)],
                                  in_=oab[:, :, :]),
                              inc="qs", amt=16)
            p.step("sync", [last], lambda sy: sy.nop())
            p.flush()
    b.barrier()


def stage_ffn_l1(pre, w):
    out = {}
    out[pre + "W1"] = np.ascontiguousarray(
        np.transpose(w["up_weights"], (0, 2, 1)))
    out[pre + "W2"] = np.ascontiguousarray(
        np.transpose(w["down_weights"], (0, 2, 1)))
    out[pre + "Ws1"] = np.ascontiguousarray(w["shared_up_weight"].T)
    out[pre + "Ws2"] = np.ascontiguousarray(w["shared_down_weight"].T)
    wr = np.zeros((D, 128), np.float32)
    wr[:, 0:8] = w["gate_weight"].T
    out[pre + "Wrpad"] = wr
    out[pre + "gbias"] = np.ascontiguousarray(
        np.tile(w["gate_bias"][None, None, :], (128, 32, 1)))
    return out


def ffn_l1_dram_specs(pre):
    return {
        pre + "W1": ([8, D, 512], F32), pre + "W2": ([8, 512, D], F32),
        pre + "Ws1": ([D, 2048], F32), pre + "Ws2": ([2048, D], F32),
        pre + "Wrpad": ([D, 128], F32), pre + "gbias": ([128, 32, 8], F32),
    }


def emit_ffn_l3(b, pre):
    nc = b.nc
    resid = b.dram["resid"].ap().rearrange("(c p) t -> p c t", p=128)
    lnd = b.dram["ln_dram"].ap().rearrange("(c p) t -> p c t", p=128)
    oacc = b.dram["oacc_dram"].ap().rearrange("(c p) t -> p c t", p=128)
    NE = 8
    TCH = 512
    TL = 2048
    NTC = TL // TCH
    hd = b.dram["h_dram"].ap()[0:64 * 128, :].rearrange(
        "(g p) t -> g p t", p=128)

    with contextlib.ExitStack() as xs:
        afm = xs.enter_context(nc.sbuf_tensor("f3_afm", [8, TL], F32))
        a01 = xs.enter_context(nc.sbuf_tensor("f3_a01", [8, 2, TL], F32))
        b1c = xs.enter_context(nc.sbuf_tensor("f3_b1c", [128, NE, 16], F32))
        p0 = P(b)
        ta = p0.step("gpsimd", [],
                     lambda g: g.dma_start(out=a01[:, 0, :],
                                           in_=b.dram[pre + "a0"].ap()),
                     inc="qg", amt=16)
        tb = p0.step("gpsimd", [],
                     lambda g: g.dma_start(out=a01[:, 1, :],
                                           in_=b.dram[pre + "a1"].ap()),
                     inc="qg", amt=16)
        tc = p0.step("gpsimd", [],
                     lambda g: g.dma_start(out=b1c[:, :, :],
                                           in_=b.dram[pre + "b1c"].ap()),
                     inc="qg", amt=16)
        p0.step("vector", [ta, tb, tc],
                lambda v: (v.tensor_add(afm[:, :], a01[:, 0, :], a01[:, 1, :]),
                           v.drain())[-1], inc="ev")
        p0.flush()
        b.barrier()

        with contextlib.ExitStack() as x2:
            xnr = x2.enter_context(nc.sbuf_tensor("f3a_xn", [128, NDC, TL], BF16))
            w1b = x2.enter_context(nc.sbuf_tensor("f3a_w1", [128, NDC, 2048], BF16))
            abce = x2.enter_context(nc.sbuf_tensor("f3a_ab", [128, NTC, TCH], F32))
            gt = x2.enter_context(nc.sbuf_tensor("f3a_gt", [128, TCH], F32))
            sg = x2.enter_context(nc.sbuf_tensor("f3a_sg", [128, TCH], F32))
            ut = x2.enter_context(nc.sbuf_tensor("f3a_ut", [128, TCH], F32))
            hstage = x2.enter_context(nc.sbuf_tensor("f3a_hs", [128, TCH], BF16))
            psA = x2.enter_context(nc.psum_tensor("f3a_psA", [128, 512], F32))
            psA2 = x2.enter_context(nc.psum_tensor("f3a_psA2", [128, 512], F32))
            psB = x2.enter_context(nc.psum_tensor("f3a_psB", [128, 512], F32))
            p = P(b)
            last = p.step("gpsimd", [],
                          lambda g: g.dma_start(out=xnr[:, :, :],
                                                in_=lnd[:, :, 0:TL]),
                          inc="qg", amt=16)
            for e in range(NE):
                w1d = b.dram[pre + "W1"].ap()[e].rearrange(
                    "(c p) f -> p c f", p=128)
                last = p.step("gpsimd", [last],
                              lambda g, _w=w1d: g.dma_start(
                                  out=w1b[:, :, :], in_=_w),
                              inc="qg", amt=16)
                for i in range(NTC):
                    mab = p.step("tensor", [last],
                                 lambda t, _i=i, _e=e: t.matmul(
                                     psB[:, :],
                                     b.consts["sel8"][:, ts(_e, 128)],
                                     afm[:, ts(_i, TCH)], start=True,
                                     stop=True), inc="mm")
                    last = p.step("vector", [mab],
                                  lambda v, _i=i: (v.tensor_copy(
                                      abce[:, _i, :], psB[:, :]),
                                      v.drain())[-1], inc="ev")
                for fc in range(8):
                    for i in range(NTC):
                        mm1 = p.step("tensor", [last],
                                     lambda t, _fc=fc, _i=i: [
                                         t.matmul(psA[:, :],
                                                  w1b[:, ic, ts(_fc, 128)],
                                                  xnr[:, ic, ts(_i, TCH)],
                                                  start=(ic == 0),
                                                  stop=(ic == NDC - 1))
                                         for ic in range(NDC)][-1], inc="mm")
                        mm2 = p.step("tensor", [mm1],
                                     lambda t, _fc=fc, _i=i: [
                                         t.matmul(psA2[:, :],
                                                  w1b[:, ic, ts(8 + _fc, 128)],
                                                  xnr[:, ic, ts(_i, TCH)],
                                                  start=(ic == 0),
                                                  stop=(ic == NDC - 1))
                                         for ic in range(NDC)][-1], inc="mm")
                        s1 = p.step("scalar", [mm1],
                                    lambda s, _e=e, _fc=fc: s.activation(
                                        gt[:, :], psA[:, :], AF.Identity,
                                        bias=b1c[:, _e, _fc:_fc + 1]),
                                    inc="es", fence=True)
                        s2 = p.step("scalar", [mm2, s1],
                                    lambda s, _e=e, _fc=fc: s.activation(
                                        ut[:, :], psA2[:, :], AF.Identity,
                                        bias=b1c[:, _e, 8 + _fc:9 + _fc]),
                                    inc="es", fence=True)

                        def clips(v):
                            v.tensor_single_scalar(gt[:, :], gt[:, :], 7.0,
                                                   ALU.min)
                            v.drain()
                            v.tensor_single_scalar(ut[:, :], ut[:, :], 7.0,
                                                   ALU.min)
                            v.drain()
                            v.tensor_single_scalar(ut[:, :], ut[:, :], -7.0,
                                                   ALU.max)
                            v.drain()
                            v.tensor_single_scalar(ut[:, :], ut[:, :], 1.0,
                                                   ALU.add)
                            return v.drain()

                        c1 = p.step("vector", [s2], clips, inc="ev")
                        a1_ = p.step("scalar", [c1],
                                     lambda s: s.activation(sg[:, :], gt[:, :],
                                                            AF.Sigmoid,
                                                            scale=1.702),
                                     inc="es", fence=True)

                        def mults(v, _i=i):
                            v.tensor_mul(gt[:, :], gt[:, :], sg[:, :])
                            v.drain()
                            v.tensor_mul(gt[:, :], gt[:, :], ut[:, :])
                            v.drain()
                            v.tensor_mul(gt[:, :], gt[:, :], abce[:, _i, :])
                            v.drain()
                            v.tensor_copy(hstage[:, :], gt[:, :])
                            return v.drain()

                        mv = p.step("vector", [a1_], mults, inc="ev")
                        last = p.step("sync", [mv],
                                      lambda sy, _g=e * 8 + fc, _i=i:
                                      sy.dma_start(
                                          out=hd[_g, :, ts(_i, TCH)],
                                          in_=hstage[:, :]),
                                      inc="qs", amt=16)
            p.step("sync", [last], lambda sy: sy.nop())
            p.flush()
        b.barrier()

        with contextlib.ExitStack() as x2:
            w2r = x2.enter_context(nc.sbuf_tensor("f3b_w2", [128, 8, D], BF16))
            hbuf = x2.enter_context(nc.sbuf_tensor("f3b_h", [128, 8, TCH], BF16))
            obuf = x2.enter_context(nc.sbuf_tensor("f3b_ob", [128, TCH], F32))
            psO = x2.enter_context(nc.psum_tensor("f3b_psO", [128, 512], F32))
            p = P(b)
            last = None
            for e in range(NE):
                w2d = b.dram[pre + "W2"].ap()[e].rearrange(
                    "(c p) d -> p c d", p=128)
                last = p.step("gpsimd", [last],
                              lambda g, _w=w2d: g.dma_start(
                                  out=w2r[:, :, :], in_=_w),
                              inc="qg", amt=16)
                for i in range(NTC):
                    def ldh(sy, _e=e, _i=i):
                        ins = None
                        for fc in range(8):
                            ins = sy.dma_start(
                                out=hbuf[:, fc, :],
                                in_=hd[_e * 8 + fc, :, ts(_i, TCH)])
                            ins.then_inc(b.h["qs"], 16)
                        return ins
                    last = p.step("sync", [last], ldh, inc="qs",
                                  amt=16 * 8, noinst=True)
                    for dc in range(NDC):
                        mm2 = p.step("tensor", [last],
                                     lambda t, _dc=dc: [
                                         t.matmul(psO[:, :],
                                                  w2r[:, fc, ts(_dc, 128)],
                                                  hbuf[:, fc, :],
                                                  start=(fc == 0),
                                                  stop=(fc == 7))
                                         for fc in range(8)][-1], inc="mm")
                        oev = p.step("vector", [mm2],
                                     lambda v: (v.tensor_copy(obuf[:, :],
                                                              psO[:, :]),
                                                v.drain())[-1], inc="ev")
                        last = p.step("gpsimd", [oev],
                                      lambda g, _e=e, _i=i, _dc=dc:
                                      g.dma_start(
                                          out=oacc[:, _dc, ts(_i, TCH)],
                                          in_=obuf[:, :],
                                          **({} if _e == 0 else
                                             {"accum_op": ALU.add})),
                                      inc="qg", amt=16)
            p.step("gpsimd", [last], lambda g: g.nop())
            p.flush()
        b.barrier()

        with contextlib.ExitStack() as x2:
            oab = x2.enter_context(nc.sbuf_tensor("f3c_oa", [128, NDC, TCH], F32))
            xb2 = x2.enter_context(nc.sbuf_tensor("f3c_x", [128, NDC, TCH], F32))
            dbr = x2.enter_context(nc.sbuf_tensor("f3c_db", [8, D], F32))
            psS = x2.enter_context(nc.psum_tensor("f3c_ps", [128, 512], F32))
            p = P(b)
            dbt = p.step("gpsimd", [],
                         lambda g: g.dma_start(out=dbr[:, :],
                                               in_=b.dram[pre + "dbrow"].ap()),
                         inc="qg", amt=16)
            last = dbt
            for i in range(NTC):
                lo = p.step("sync", [last],
                            lambda sy, _i=i: sy.dma_start(
                                out=oab[:, :, :], in_=oacc[:, :, ts(_i, TCH)]),
                            inc="qs", amt=16)
                lx = p.step("sync", [lo],
                            lambda sy, _i=i: sy.dma_start(
                                out=xb2[:, :, :], in_=resid[:, :, ts(_i, TCH)]),
                            inc="qs", amt=16)
                last = lx
                for dc in range(NDC):
                    mb = p.step("tensor", [last],
                                lambda t, _i=i, _dc=dc: t.matmul(
                                    psS[:, :], dbr[0:8, ts(_dc, 128)],
                                    afm[0:8, ts(_i, TCH)],
                                    start=True, stop=True), inc="mm")
                    last = p.step("vector", [mb],
                                  lambda v, _dc=dc: (
                                      v.tensor_add(oab[:, _dc, :],
                                                   oab[:, _dc, :], psS[:, :]),
                                      v.drain(),
                                      v.tensor_add(oab[:, _dc, :],
                                                   oab[:, _dc, :],
                                                   xb2[:, _dc, :]),
                                      v.drain())[-1], inc="ev")
                last = p.step("sync", [last],
                              lambda sy, _i=i: sy.dma_start(
                                  out=resid[:, :, ts(_i, TCH)],
                                  in_=oab[:, :, :]),
                              inc="qs", amt=16)
            p.step("sync", [last], lambda sy: sy.nop())
            p.flush()
    b.barrier()


def stage_ffn_l3(pre, w):
    out = {}
    gup = w["gate_up_proj"]
    w1p = np.concatenate([gup[:, :, 0::2], gup[:, :, 1::2]], axis=2)
    out[pre + "W1"] = np.ascontiguousarray(w1p)
    b1 = w["gate_up_proj_bias"]
    b1p = np.concatenate([b1[:, 0::2], b1[:, 1::2]], axis=1)
    b1c = np.zeros((128, 8, 16), np.float32)
    for e in range(8):
        b1c[:, e, :] = b1p[e].reshape(16, 128).T
    out[pre + "b1c"] = b1c
    out[pre + "W2"] = np.ascontiguousarray(w["down_proj"])
    out[pre + "dbrow"] = np.ascontiguousarray(w["down_proj_bias"])
    return out


def stage_ffn_l3_alpha(pre, inp, core):
    idx = inp["L3_ffn_router_indices"]
    rw = inp["L3_ffn_routing_weights"]
    TL = 2048
    res = {}
    for j in range(2):
        a = np.zeros((8, TL), np.float32)
        if core == 0:
            tt = np.arange(TL)
            a[idx[:, j], tt] = rw[tt, idx[:, j]]
        res[pre + f"a{j}"] = a
    return res


def ffn_l3_dram_specs(pre):
    return {
        pre + "W1": ([8, D, 2048], F32), pre + "W2": ([8, D, D], F32),
        pre + "b1c": ([128, 8, 16], F32), pre + "dbrow": ([8, D], F32),
        pre + "a0": ([8, 2048], F32), pre + "a1": ([8, 2048], F32),
    }


def emit_attn_l1b(b, pre):
    nc = b.nc
    resid = b.dram["resid"].ap().rearrange("(c p) t -> p c t", p=128)
    lnd = b.dram["ln_dram"].ap().rearrange("(c p) t -> p c t", p=128)

    with contextlib.ExitStack() as xs:
        qpb = xs.enter_context(nc.sbuf_tensor("1b_q", [128, 4, T], BF16))
        kb = xs.enter_context(nc.sbuf_tensor("1b_k", [128, 4, 5, 512], BF16))
        vb = xs.enter_context(nc.sbuf_tensor("1b_v", [128, NDC, 5, 512], BF16))
        sm = xs.enter_context(nc.sbuf_tensor("1b_sm", [128, NDC, 4], F32))
        wpc = xs.enter_context(nc.sbuf_tensor("1b_wp", [128, 8, 5], F32))
        p0 = P(b)
        t1 = p0.step("gpsimd", [],
                     lambda g: g.dma_start(out=sm[:, :, :],
                                           in_=b.dram[pre + "smalls"].ap()),
                     inc="qg", amt=16)
        t2 = p0.step("gpsimd", [],
                     lambda g: g.dma_start(out=wpc[:, :, :],
                                           in_=b.dram[pre + "wpc"].ap()),
                     inc="qg", amt=16)
        p0.step("vector", [t1, t2], lambda v: v.engine_nop())
        p0.flush()
        b.barrier()

        with contextlib.ExitStack() as x2:
            xfull = x2.enter_context(nc.sbuf_tensor("1b_x", [128, NDC, T], BF16))
            wqb = x2.enter_context(nc.sbuf_tensor("1b_wq", [128, NDC, 512], BF16))
            qt = x2.enter_context(nc.sbuf_tensor("1b_qt", [128, 512], F32))
            vacc = x2.enter_context(nc.sbuf_tensor("1b_va", [128, 5, 512], F32))
            psq = x2.enter_context(nc.psum_tensor("1b_psq", [128, 512], F32))
            p = P(b)
            lx = p.step("gpsimd", [],
                        lambda g: g.dma_start(out=xfull[:, :, :],
                                              in_=lnd[:, :, :]),
                        inc="qg", amt=16)
            lw = p.step("gpsimd", [],
                        lambda g: g.dma_start(
                            out=wqb[:, :, :],
                            in_=b.dram[pre + "Wqp"].ap()
                            .rearrange("(c p) o -> p c o", p=128)),
                        inc="qg", amt=16)
            last = p.step("vector", [lx, lw], lambda v: v.engine_nop(),
                          inc="ev")
            for i in range(8):
                for oc in range(4):
                    mm = p.step("tensor", [last],
                                lambda t, _i=i, _oc=oc: [
                                    t.matmul(psq[:, :],
                                             wqb[:, ic, ts(_oc, 128)],
                                             xfull[:, ic, ts(_i, 512)],
                                             start=(ic == 0),
                                             stop=(ic == NDC - 1))
                                    for ic in range(NDC)][-1], inc="mm")
                    sc = p.step("scalar", [mm],
                                lambda s, _oc=oc: s.activation(
                                    qt[:, :], psq[:, :], AF.Identity,
                                    bias=sm[:, _oc, 0:1]),
                                inc="es", fence=True)
                    last = p.step("vector", [sc],
                                  lambda v, _i=i, _oc=oc: (
                                      v.tensor_copy(
                                          qpb[:, _oc, ts(_i, 512)], qt[:, :]),
                                      v.drain())[-1], inc="ev")

            def kmath(v):
                for dc in range(4):
                    for j in range(5):
                        v.tensor_scalar(vacc[:, j, :],
                                        qpb[:, dc, 0:512],
                                        wpc[:, 0, j:j + 1], None, ALU.mult)
                        v.drain()
                        for l in range(1, 8):
                            v.scalar_tensor_tensor(
                                vacc[:, j, :], qpb[:, dc, ts(l, 512)],
                                wpc[:, l, j:j + 1], vacc[:, j, :],
                                ALU.mult, ALU.add)
                            v.drain()
                    v.tensor_copy(kb[:, dc, :, :], vacc[:, :, :])
                    v.drain()
                return v.drain()

            last = p.step("vector", [last], kmath, inc="ev")

            def vmath(v):
                for dc in range(NDC):
                    for j in range(5):
                        v.tensor_scalar(vacc[:, j, :],
                                        xfull[:, dc, 0:512],
                                        wpc[:, 0, j:j + 1], None, ALU.mult)
                        v.drain()
                        for l in range(1, 8):
                            v.scalar_tensor_tensor(
                                vacc[:, j, :], xfull[:, dc, ts(l, 512)],
                                wpc[:, l, j:j + 1], vacc[:, j, :],
                                ALU.mult, ALU.add)
                            v.drain()
                    v.tensor_copy(vb[:, dc, :, :], vacc[:, :, :])
                    v.drain()
                return v.drain()

            last = p.step("vector", [last], vmath, inc="ev")
            p.step("vector", [last], lambda v: v.engine_nop())
            p.flush()
        b.barrier()

        with contextlib.ExitStack() as x2:
            prodb = x2.enter_context(nc.sbuf_tensor("1b_pr", [128, 4, 512], BF16))
            Sall = x2.enter_context(nc.sbuf_tensor("1b_S", [16, 8, 5, 512], BF16))
            dsum = x2.enter_context(nc.sbuf_tensor("1b_d", [16, 8, 512], F32))
            abt = x2.enter_context(nc.sbuf_tensor("1b_ab", [128, 512], BF16))
            tmpv = x2.enter_context(nc.sbuf_tensor("1b_tv", [128, 512], F32))
            oaccv = x2.enter_context(nc.sbuf_tensor("1b_oc", [128, NDC, 512], F32))
            oacb = x2.enter_context(nc.sbuf_tensor("1b_ob", [128, NDC, 512], BF16))
            xrb = x2.enter_context(nc.sbuf_tensor("1b_xr", [128, 2, 512], F32))
            wot = x2.enter_context(nc.sbuf_tensor("1b_wo", [128, NDC, 512], BF16))
            psS = x2.enter_context(nc.psum_tensor("1b_psS", [16, 512], F32))
            psB = x2.enter_context(nc.psum_tensor("1b_psB", [128, 512], F32))
            psO = x2.enter_context(nc.psum_tensor("1b_psO", [128, 512], F32))
            p = P(b)
            last = None
            for l in range(8):
                for j in range(5):
                    pm = p.step("vector", [last],
                                lambda v, _l=l, _j=j: (
                                    [v.tensor_mul(prodb[:, dc, :],
                                                  qpb[:, dc, ts(_l, 512)],
                                                  kb[:, dc, _j, :])
                                     for dc in range(4)],
                                    v.drain())[-1], inc="ev")
                    mm = p.step("tensor", [pm],
                                lambda t, _j=j: [
                                    t.matmul(psS[:, :],
                                             b.consts["msk16"][:, :],
                                             prodb[:, dc, :],
                                             start=(dc == 0), stop=(dc == 3))
                                    for dc in range(4)][-1], inc="mm")
                    last = p.step("vector", [mm],
                                  lambda v, _l=l, _j=j: (
                                      v.tensor_copy(
                                          Sall[:, _l, _j, :],
                                          psS[:, :]),
                                      v.drain())[-1], inc="ev")
            ex = p.step("scalar", [last],
                        lambda s: s.activation(Sall[:, :, :, :],
                                               Sall[:, :, :, :],
                                               AF.Exp, scale=0.125),
                        inc="es", fence=True)

            def smax(v):
                for l in range(8):
                    v.tensor_copy(dsum[:, l, :], Sall[:, l, 0, :])
                    v.drain()
                    for j in range(1, 5):
                        v.tensor_add(dsum[:, l, :], dsum[:, l, :],
                                     Sall[:, l, j, :])
                        v.drain()
                v.reciprocal(dsum[:, :, :], dsum[:, :, :])
                v.drain()
                for l in range(8):
                    for j in range(5):
                        v.tensor_mul(Sall[:, l, j, :], Sall[:, l, j, :],
                                     dsum[:, l, :])
                        v.drain()
                return v.drain()

            last = p.step("vector", [ex], smax, inc="ev")
            for l in range(8):
                for j in range(5):
                    mm = p.step("tensor", [last],
                                lambda t, _l=l, _j=j: t.matmul(
                                    psB[:, :], b.consts["b16c"][:, :],
                                    Sall[:, _l, _j, :], start=True,
                                    stop=True),
                                inc="mm")

                    def wsum(v, _j=j):
                        v.tensor_copy(abt[:, :], psB[:, :])
                        v.drain()
                        for dc in range(NDC):
                            if _j == 0:
                                v.tensor_mul(oaccv[:, dc, :], abt[:, :],
                                             vb[:, dc, _j, :])
                            else:
                                v.tensor_mul(tmpv[:, :], abt[:, :],
                                             vb[:, dc, _j, :])
                                v.drain()
                                v.tensor_add(oaccv[:, dc, :],
                                             oaccv[:, dc, :], tmpv[:, :])
                            v.drain()
                        return v.drain()

                    last = p.step("vector", [mm], wsum, inc="ev")
                cst = p.step("vector", [last],
                             lambda v: (v.tensor_copy(oacb[:, :, :],
                                                      oaccv[:, :, :]),
                                        v.drain())[-1], inc="ev")
                last = cst
                for oc in range(NDC):
                    ob = oc % 2
                    if oc % 4 == 0:
                        last = p.step("gpsimd", [last],
                                      lambda g, _h=oc // 4: g.dma_start(
                                          out=wot[:, :, :],
                                          in_=b.dram[pre + "WoT"].ap()
                                          .rearrange("(c p) o -> p c o",
                                                     p=128)
                                          [:, :, ts(_h, 512)]),
                                      inc="qg", amt=16)
                    mo = p.step("tensor", [last],
                                lambda t, _oc=oc: [
                                    t.matmul(psO[:, :],
                                             wot[:, ic, ts(_oc % 4, 128)],
                                             oacb[:, ic, :],
                                             start=(ic == 0),
                                             stop=(ic == NDC - 1))
                                    for ic in range(NDC)][-1], inc="mm")
                    lxf = p.step("sync", [mo],
                                 lambda sy, _l=l, _oc=oc, _ob=ob:
                                 sy.dma_start(
                                     out=xrb[:, _ob, :],
                                     in_=resid[:, _oc, ts(_l, 512)]),
                                 inc="qs", amt=16)
                    ad = p.step("vector", [lxf],
                                lambda v, _oc=oc, _ob=ob: (
                                    v.scalar_tensor_tensor(
                                        xrb[:, _ob, :], psO[:, :],
                                        sm[:, _oc, 1:2], xrb[:, _ob, :],
                                        ALU.add, ALU.add),
                                    v.drain())[-1], inc="ev")
                    last = p.step("sync", [ad],
                                  lambda sy, _l=l, _oc=oc, _ob=ob:
                                  sy.dma_start(
                                      out=resid[:, _oc, ts(_l, 512)],
                                      in_=xrb[:, _ob, :]),
                                  inc="qs", amt=16)
            p.step("sync", [last], lambda sy: sy.nop())
            p.flush()
    b.barrier()


def stage_attn_l1b(pre, w):
    psi = np.array([(c % 16) * 32 + c // 16 for c in range(512)])
    out = {}
    out[pre + "Wqp"] = np.ascontiguousarray(w["w_q"][psi, :].T)
    sm = np.zeros((128, NDC, 4), np.float32)
    bqp = np.zeros(D, np.float32)
    bqp[0:512] = w["b_q"][psi]
    sm[:, :, 0] = col_of(bqp)[:, :, 0]
    sm[:, :, 1] = col_of(w["b_out"])[:, :, 0]
    out[pre + "smalls"] = sm
    wpc = np.zeros((128, 8, 5), np.float32)
    for l in range(8):
        for j in range(5):
            wpc[:, l, j] = w["w_kv"][j, l]
    out[pre + "wpc"] = wpc
    out[pre + "WoT"] = np.ascontiguousarray(w["w_out"].T)
    return out


def attn_l1b_dram_specs(pre):
    return {
        pre + "Wqp": ([D, 512], F32), pre + "smalls": ([128, NDC, 4], F32),
        pre + "wpc": ([128, 8, 5], F32), pre + "WoT": ([D, D], F32),
    }


CONST_SPECS["msk16"] = ([128, 16], BF16)
CONST_SPECS["b16c"] = ([16, 128], BF16)
_hc_orig = host_consts


def host_consts():
    c = _hc_orig()
    m = np.zeros((128, 16), np.float32)
    for p_ in range(128):
        m[p_, p_ % 16] = 1.0
    c["msk16"] = m
    c["b16c"] = np.ascontiguousarray(m.T)
    return c


def stage_attn_l2b(pre, w):
    out = {
        pre + "Wq": np.ascontiguousarray(w["wq"]),
        pre + "Wk": np.ascontiguousarray(w["wk"]),
        pre + "Wv": np.ascontiguousarray(w["wv"]),
        pre + "Wo": np.ascontiguousarray(w["wo"]),
        pre + "Wfq": np.ascontiguousarray(w["wfq"]),
        pre + "Wfg": np.ascontiguousarray(w["wfg"]),
    }
    smalls = np.zeros((128, NDC, 128), np.float32)
    smalls[:, :, 0] = col_of(w["bo"])[:, :, 0]
    smalls[:, :, 1] = col_of(w["bq"])[:, :, 0]
    smalls[:, :, 2] = col_of(w["bk"])[:, :, 0]
    smalls[:, :, 3] = col_of(w["bfq"])[:, :, 0]
    smalls[:, :, 4] = col_of(w["bfg"])[:, :, 0]
    out[pre + "smalls"] = smalls
    out[pre + "bvrow"] = np.tile(w["bv"][None, :], (128, 1))
    upt = np.zeros((128, NDC, 32), np.float32)
    for dc in range(NDC):
        for p_ in range(128):
            f = dc * 128 + p_
            h = f // 64
            upt[p_, dc, h] = w["up"][h, 0, f % 64]
            upt[p_, dc, 16 + h] = w["uz"][h, 0, f % 64]
    out[pre + "uptile"] = upt
    kk = np.arange(T, dtype=np.float64)
    karr = np.stack([kk * kk, kk, np.ones(T)], axis=0).astype(np.float32)
    out[pre + "karr"] = karr
    return out


def attn_l2b_dram_specs(pre):
    sp = attn_dram_specs(pre)
    sp[pre + "Wfq"] = ([D, D], F32)
    sp[pre + "Wfg"] = ([D, D], F32)
    sp[pre + "uptile"] = ([128, NDC, 32], F32)
    sp[pre + "karr"] = ([3, T], F32)
    return sp


def emit_attn_l3a(b, pre):
    nc = b.nc
    resid = b.dram["resid"].ap().rearrange("(c p) t -> p c t", p=128)
    lnd = b.dram["ln_dram"].ap().rearrange("(c p) t -> p c t", p=128)
    qd = b.dram["q_dram"].ap().rearrange("(c p) t -> p c t", p=128)
    vfd = b.dram["xn_dram"].ap().rearrange("(c p) t -> p c t", p=128)
    TCH = 512
    NTC = T // TCH
    NH = 16

    with contextlib.ExitStack() as xs:
        Kall = xs.enter_context(nc.sbuf_tensor("3a_K", [128, NDC, T], BF16))
        Kv = Kall.rearrange("p c (u v) -> p c u v", v=16)

        for kind in ("q", "k"):
            wn = ("Wq", "Wqsw") if kind == "q" else ("Wk", "Wksw")
            with contextlib.ExitStack() as x2:
                wA = x2.enter_context(nc.sbuf_tensor("3a_wa" + kind, [128, NDC, D], BF16))
                wB = x2.enter_context(nc.sbuf_tensor("3a_wb" + kind, [128, NDC, D], BF16))
                lnb = x2.enter_context(nc.sbuf_tensor("3a_ln" + kind, [128, NDC, TCH], BF16))
                rc3 = x2.enter_context(nc.sbuf_tensor("3a_rc" + kind, [128, NDC, TCH], F32))
                rs3 = x2.enter_context(nc.sbuf_tensor("3a_rs" + kind, [128, NDC, TCH], F32))
                t1t = x2.enter_context(nc.sbuf_tensor("3a_t1" + kind, [128, TCH], F32))
                t2t = x2.enter_context(nc.sbuf_tensor("3a_t2" + kind, [128, TCH], F32))
                qsf = x2.enter_context(nc.sbuf_tensor("3a_qs" + kind, [128, NDC, TCH], F32))
                stg = x2.enter_context(nc.sbuf_tensor("3a_st" + kind, [128, T], BF16))
                psq = x2.enter_context(nc.psum_tensor("3a_psA" + kind, [128, 512], F32))
                psq2 = x2.enter_context(nc.psum_tensor("3a_psB" + kind, [128, 512], F32))
                stgv = stg.rearrange("p (u v) -> p u v", v=16)
                p = P(b)
                tA = p.step("gpsimd", [],
                            lambda g, _n=wn[0]: g.dma_start(
                                out=wA[:, :, :],
                                in_=b.dram[pre + _n].ap()
                                .rearrange("(c p) o -> p c o", p=128)),
                            inc="qg", amt=16)
                tB = p.step("gpsimd", [],
                            lambda g, _n=wn[1]: g.dma_start(
                                out=wB[:, :, :],
                                in_=b.dram[pre + _n].ap()
                                .rearrange("(c p) o -> p c o", p=128)),
                            inc="qg", amt=16)
                last = p.step("vector", [tA, tB], lambda v: v.engine_nop(),
                              inc="ev")
                for i in range(NTC):
                    l1 = p.step("sync", [last],
                                lambda sy, _i=i: sy.dma_start(
                                    out=lnb[:, :, :],
                                    in_=lnd[:, :, ts(_i, TCH)]),
                                inc="qs", amt=16)
                    l2 = p.step("sync", [l1],
                                lambda sy, _i=i: sy.dma_start(
                                    out=rc3[:, :, :],
                                    in_=b.dram[pre + "c3"].ap()
                                    [:, :, ts(_i, TCH)]),
                                inc="qs", amt=16)
                    l3 = p.step("sync", [l2],
                                lambda sy, _i=i: sy.dma_start(
                                    out=rs3[:, :, :],
                                    in_=b.dram[pre + "s3"].ap()
                                    [:, :, ts(_i, TCH)]),
                                inc="qs", amt=16)
                    last = l3
                    for oc in range(NDC):
                        mmA = p.step("tensor", [last],
                                     lambda t, _oc=oc: [
                                         t.matmul(psq[:, :],
                                                  wA[:, ic, ts(_oc, 128)],
                                                  lnb[:, ic, :],
                                                  start=(ic == 0),
                                                  stop=(ic == NDC - 1))
                                         for ic in range(NDC)][-1], inc="mm")
                        mmB = p.step("tensor", [mmA],
                                     lambda t, _oc=oc: [
                                         t.matmul(psq2[:, :],
                                                  wB[:, ic, ts(_oc, 128)],
                                                  lnb[:, ic, :],
                                                  start=(ic == 0),
                                                  stop=(ic == NDC - 1))
                                         for ic in range(NDC)][-1], inc="mm")

                        def rope(v, _oc=oc):
                            v.tensor_mul(t1t[:, :], psq[:, :], rc3[:, _oc, :])
                            v.tensor_mul(t2t[:, :], psq2[:, :],
                                         rs3[:, _oc, :])
                            v.drain()
                            v.tensor_add(qsf[:, _oc, :], t1t[:, :],
                                         t2t[:, :])
                            return v.drain()

                        last = p.step("vector", [mmB], rope, inc="ev")

                    def scram(v, _i=i, _kind=kind):
                        for hh in range(2):
                            for vv in range(16):
                                src = qsf[(vv % 2) * 64:(vv % 2) * 64 + 64,
                                          vv // 2,
                                          hh * 256:(hh + 1) * 256]
                                if _kind == "q":
                                    dst = stgv[hh * 64:hh * 64 + 64, :, vv]
                                else:
                                    dst = Kv[hh * 64:hh * 64 + 64, _i, :, vv]
                                v.tensor_copy(dst, src)
                        return v.drain()

                    last = p.step("vector", [last], scram, inc="ev")
                    if kind == "q":
                        last = p.step("sync", [last],
                                      lambda sy, _i=i: sy.dma_start(
                                          out=qd[:, _i, :], in_=stg[:, :]),
                                      inc="qs", amt=16)
                p.step("sync", [last], lambda sy: sy.nop())
                p.flush()
            b.barrier()

        with contextlib.ExitStack() as x2:
            wv = x2.enter_context(nc.sbuf_tensor("3a_wv", [128, NDC, D], BF16))
            lnb = x2.enter_context(nc.sbuf_tensor("3a_lnv", [128, NDC, TCH], BF16))
            qsf = x2.enter_context(nc.sbuf_tensor("3a_vs", [128, NDC, TCH], F32))
            stg = x2.enter_context(nc.sbuf_tensor("3a_vst", [128, T], BF16))
            psv = x2.enter_context(nc.psum_tensor("3a_psv", [128, 512], F32))
            stgv = stg.rearrange("p (u v) -> p u v", v=16)
            p = P(b)
            tV = p.step("gpsimd", [],
                        lambda g: g.dma_start(
                            out=wv[:, :, :],
                            in_=b.dram[pre + "Wv"].ap()
                            .rearrange("(c p) o -> p c o", p=128)),
                        inc="qg", amt=16)
            last = tV
            for i in range(NTC):
                l1 = p.step("sync", [last],
                            lambda sy, _i=i: sy.dma_start(
                                out=lnb[:, :, :], in_=lnd[:, :, ts(_i, TCH)]),
                            inc="qs", amt=16)
                last = l1
                for oc in range(NDC):
                    mmV = p.step("tensor", [last],
                                 lambda t, _oc=oc: [
                                     t.matmul(psv[:, :],
                                              wv[:, ic, ts(_oc, 128)],
                                              lnb[:, ic, :],
                                              start=(ic == 0),
                                              stop=(ic == NDC - 1))
                                     for ic in range(NDC)][-1], inc="mm")
                    last = p.step("vector", [mmV],
                                  lambda v, _oc=oc: (
                                      v.tensor_copy(qsf[:, _oc, :],
                                                    psv[:, :]),
                                      v.drain())[-1], inc="ev")

                def scramv(v, _i=i):
                    for hh in range(2):
                        for vv in range(16):
                            v.tensor_copy(
                                stgv[hh * 64:hh * 64 + 64, :, vv],
                                qsf[(vv % 2) * 64:(vv % 2) * 64 + 64,
                                    vv // 2, hh * 256:(hh + 1) * 256])
                    return v.drain()

                last = p.step("vector", [last], scramv, inc="ev")
                last = p.step("sync", [last],
                              lambda sy, _i=i: sy.dma_start(
                                  out=vfd[:, _i, :], in_=stg[:, :]),
                              inc="qs", amt=16)
            p.step("sync", [last], lambda sy: sy.nop())
            p.flush()
        b.barrier()

        Vtm = xs.enter_context(nc.sbuf_tensor("3a_V", [128, 512 * 65], BF16))
        Vv = Vtm.rearrange("p (s c) -> p s c", c=65)
        with contextlib.ExitStack() as x2:
            vch = x2.enter_context(nc.sbuf_tensor("3a_vch", [128, T], BF16))
            pt128 = x2.enter_context(nc.psum_tensor("3a_pt", [128, 64], BF16))
            p = P(b)
            z1 = p.step("vector", [],
                        lambda v: (v.memset(Vv[:, :, 64:65], 1.0),
                                   v.drain())[-1], inc="ev")
            last = z1
            for hc in range(NDC):
                lv = p.step("sync", [last],
                            lambda sy, _hc=hc: sy.dma_start(
                                out=vch[:, :], in_=vfd[:, _hc, :]),
                            inc="qs", amt=16)
                last = lv
                for hh in range(2):
                    h = hc * 2 + hh
                    for kt in range(32):
                        tr = p.step("tensor", [last],
                                    lambda t, _hh=hh, _kt=kt: t.transpose(
                                        pt128[:, :],
                                        vch[_hh * 64:_hh * 64 + 64,
                                            ts(_kt, 128)],
                                        b.consts["identb"]
                                        [_hh * 64:_hh * 64 + 64,
                                         _hh * 64:_hh * 64 + 64]),
                                    inc="mm")
                        last = p.step("vector", [tr],
                                      lambda v, _h=h, _kt=kt: (
                                          v.tensor_copy(
                                              Vv[:, _h * 32 + _kt, 0:64],
                                              pt128[:, :]),
                                          v.drain())[-1], inc="ev")
            p.step("vector", [last], lambda v: v.engine_nop())
            p.flush()
        b.barrier()

        with contextlib.ExitStack() as x2:
            wrt = x2.enter_context(nc.sbuf_tensor("3a_wr", [128, NDC, D], BF16))
            qch = x2.enter_context(nc.sbuf_tensor("3a_qc", [128, NDC, TCH], BF16))
            expS = x2.enter_context(nc.sbuf_tensor("3a_eS", [128, TCH], BF16))
            sgr = x2.enter_context(nc.sbuf_tensor("3a_sg", [1, TCH], F32))
            gbt = x2.enter_context(nc.sbuf_tensor("3a_gb", [128, TCH], BF16))
            ctxc = x2.enter_context(nc.sbuf_tensor("3a_cx", [128, NDC, 1024], BF16))
            ctm = x2.enter_context(nc.sbuf_tensor("3a_ctm", [128, NDC, 1024], BF16))
            fout = x2.enter_context(nc.sbuf_tensor("3a_fo", [128, D], BF16))
            xrb = x2.enter_context(nc.sbuf_tensor("3a_xr", [128, 2, 128], F32))
            psS = x2.enter_context(nc.psum_tensor("3a_psS", [128, 512], F32))
            psC = x2.enter_context(nc.psum_tensor("3a_psC", [128, 512], F32))
            psB = x2.enter_context(nc.psum_tensor("3a_psB", [128, 512], F32))
            psO = x2.enter_context(nc.psum_tensor("3a_psO", [128, 512], F32))
            psT = x2.enter_context(nc.psum_tensor("3a_psT", [128, 128], BF16))
            p = P(b)
            tW = p.step("gpsimd", [],
                        lambda g: g.dma_start(
                            out=wrt[:, :, :],
                            in_=b.dram[pre + "WrT"].ap()
                            .rearrange("(c p) o -> p c o", p=128)),
                        inc="qg", amt=16)
            last = tW
            for m in range(4):
                for q2 in range(2):
                    i = m * 2 + q2
                    lq = p.step("sync", [last],
                                lambda sy, _i=i: sy.dma_start(
                                    out=qch[:, :, :], in_=qd[:, :, ts(_i, TCH)]),
                                inc="qs", amt=16)
                    last = lq
                    for h in range(NH):
                        hb = (h % 2) * 64
                        hc = h // 2
                        for kt in range(32):
                            mms = p.step("tensor", [last],
                                         lambda t, _hb=hb, _hc=hc, _kt=kt:
                                         t.matmul(
                                             psS[:, :],
                                             Kall[_hb:_hb + 64, _hc,
                                                  ts(_kt, 128)],
                                             qch[_hb:_hb + 64, _hc, :],
                                             start=True, stop=True),
                                         inc="mm")
                            exk = p.step("scalar", [mms],
                                         lambda s: s.activation(
                                             expS[:, :], psS[:, :], AF.Exp,
                                             scale=0.25),
                                         inc="es", fence=True)
                            mmc = p.step("tensor", [exk],
                                         lambda t, _h=h, _kt=kt,
                                         _st=(kt == 0), _sp=(kt == 31):
                                         t.matmul(psC[0:65, :],
                                                  Vv[:, _h * 32 + _kt, 0:65],
                                                  expS[:, :],
                                                  start=_st, stop=_sp),
                                         inc="mm")
                            last = mmc
                        rc = p.step("vector", [last],
                                    lambda v: (v.reciprocal(sgr[:, :],
                                                            psC[64:65, :]),
                                               v.drain())[-1], inc="ev")
                        mmb = p.step("tensor", [rc],
                                     lambda t: t.matmul(
                                         psB[:, :],
                                         b.consts["ones_rowf"][:, :],
                                         sgr[:, :], start=True, stop=True),
                                     inc="mm")
                        last = p.step("vector", [mmb],
                                      lambda v, _hb=hb, _hc=hc, _q2=q2: (
                                          v.tensor_copy(gbt[:, :], psB[:, :]),
                                          v.drain(),
                                          v.tensor_mul(
                                              ctxc[_hb:_hb + 64, _hc,
                                                   ts(_q2, 512)],
                                              psC[0:64, :],
                                              gbt[_hb:_hb + 64, :]),
                                          v.drain())[-1], inc="ev")
                for sc in range(8):
                    for fc in range(NDC):
                        tr = p.step("tensor", [last],
                                    lambda t, _sc=sc, _fc=fc: t.transpose(
                                        psT[:, :],
                                        ctxc[:, _fc, ts(_sc, 128)],
                                        b.consts["identb"][:, :]), inc="mm")
                        last = p.step("vector", [tr],
                                      lambda v, _sc=sc, _fc=fc: (
                                          v.tensor_copy(
                                              ctm[:, _sc, ts(_fc, 128)],
                                              psT[:, :]),
                                          v.drain())[-1], inc="ev")
                for tc in range(8):
                    for fh in range(2):
                        mmw = p.step("tensor", [last],
                                     lambda t, _tc=tc, _fh=fh: [
                                         t.matmul(psO[:, :],
                                                  wrt[:, sc, ts(_tc, 128)],
                                                  ctm[:, sc, ts(_fh, 512)],
                                                  start=(sc == 0),
                                                  stop=(sc == 7))
                                         for sc in range(8)][-1], inc="mm")
                        last = p.step("vector", [mmw],
                                      lambda v, _fh=fh: (
                                          v.tensor_copy(
                                              fout[:, ts(_fh, 512)],
                                              psO[:, :]),
                                          v.drain())[-1], inc="ev")
                    for fc in range(NDC):
                        fb = fc % 2
                        tr = p.step("tensor", [last],
                                    lambda t, _fc=fc: t.transpose(
                                        psT[:, :], fout[:, ts(_fc, 128)],
                                        b.consts["identb"][:, :]), inc="mm")
                        lxf = p.step("sync", [last],
                                     lambda sy, _m=m, _tc=tc, _fc=fc,
                                     _fb=fb: sy.dma_start(
                                         out=xrb[:, _fb, :],
                                         in_=resid[:, _fc,
                                                   _m * 1024 + _tc * 128:
                                                   _m * 1024 + _tc * 128
                                                   + 128]),
                                     inc="qs", amt=16)
                        ad = p.step("vector", [tr, lxf],
                                    lambda v, _fb=fb: (
                                        v.tensor_add(xrb[:, _fb, :],
                                                     xrb[:, _fb, :],
                                                     psT[:, :]),
                                        v.drain())[-1], inc="ev")
                        last = p.step("sync", [ad],
                                      lambda sy, _m=m, _tc=tc, _fc=fc,
                                      _fb=fb: sy.dma_start(
                                          out=resid[:, _fc,
                                                    _m * 1024 + _tc * 128:
                                                    _m * 1024 + _tc * 128
                                                    + 128],
                                          in_=xrb[:, _fb, :]),
                                      inc="qs", amt=16)
            p.step("sync", [last], lambda sy: sy.nop())
            p.flush()
    b.barrier()


def stage_attn_l3a(pre, w):
    perm = np.arange(D) ^ 1
    out = {
        pre + "Wq": np.ascontiguousarray(w["Wq"].T),
        pre + "Wqsw": np.ascontiguousarray(w["Wq"][perm, :].T),
        pre + "Wk": np.ascontiguousarray(w["Wk"].T),
        pre + "Wksw": np.ascontiguousarray(w["Wk"][perm, :].T),
        pre + "Wv": np.ascontiguousarray(w["Wv"].T),
        pre + "WrT": np.ascontiguousarray(w["Wr"].T),
    }
    c3 = np.zeros((128, NDC, T), np.float32)
    s3 = np.zeros((128, NDC, T), np.float32)
    tpos = np.arange(T)
    u = (tpos % 256).astype(np.float64)
    for p_ in range(128):
        dpr = p_ % 64
        for dc in range(NDC):
            if dpr >= 32:
                c3[p_, dc, :] = 1.0
                s3[p_, dc, :] = 0.0
                continue
            i2 = dpr // 2
            f = 10000.0 ** (-2.0 * i2 / 32.0)
            sp = u * 16.0 + dc * 2 + p_ // 64
            c3[p_, dc, :] = np.cos(sp * f)
            sgn = -1.0 if dpr % 2 == 0 else 1.0
            s3[p_, dc, :] = sgn * np.sin(sp * f)
    out[pre + "c3"] = c3
    out[pre + "s3"] = s3
    return out


def attn_l3a_dram_specs(pre):
    return {
        pre + "Wq": ([D, D], F32), pre + "Wqsw": ([D, D], F32),
        pre + "Wk": ([D, D], F32), pre + "Wksw": ([D, D], F32),
        pre + "Wv": ([D, D], F32), pre + "WrT": ([D, D], F32),
        pre + "c3": ([128, NDC, T], F32), pre + "s3": ([128, NDC, T], F32),
    }


def _w_of(inputs, prefix):
    return {k[len(prefix):]: inputs[k] for k in inputs if k.startswith(prefix)}


def _groupA(inputs):
    specs = base_dram_specs()
    staged = host_const_inputs()
    subs = [(emit_load_resid, {})]
    ln = (emit_ln_pass, {})

    w = _w_of(inputs, 'L0_attn_a_')
    specs.update({k: v + ('in',) for k, v in rwkv_dram_specs('r0_').items()})
    staged.update(stage_rwkv('r0_', w))
    subs += [ln, (emit_rwkv, {'pre': 'r0_'}), ('SNAP', 1)]

    w = _w_of(inputs, 'L0_ffn_')
    specs.update({k: v + ('in',) for k, v in ffn_l0_dram_specs('f0_').items()})
    staged.update(stage_ffn_l0('f0_', w))
    ffn0 = [ln, (emit_ln_pass, {'src': 'ln_dram', 'dst': 'xn_dram',
                                'gb': 'f0_smalls'}),
            (emit_ffn_moe, {'pre': 'f0_', 'mode': 'l0', 'NF': 32})]
    subs += ffn0 + [('SNAP', 2)]

    w = _w_of(inputs, 'L0_attn_b_')
    specs.update({k: v + ('in',) for k, v in attn_dram_specs('a0b_').items()})
    staged.update(stage_attn_std('a0b_', w, bo='bo'))
    subs += [ln, (emit_attn, {'pre': 'a0b_',
                              'cfg': {'causal': False, 'scale': 0.125}})]
    subs += [('SNAP', 3)] + ffn0 + [('SNAP', 4)]

    w = _w_of(inputs, 'L1_attn_a_')
    specs.update({k: v + ('in',)
                  for k, v in attn_l1a_dram_specs('a1a_').items()})
    specs['a1a_gdram'] = ([16, T], F32, 'internal')
    staged.update(stage_attn_l1a('a1a_', w))
    subs += [ln, (emit_attn, {'pre': 'a1a_',
                              'cfg': {'causal': False, 'scale': 0.125,
                                      'rms': True, 'gate': True}})]
    subs += [('SNAP', 5)]

    w = _w_of(inputs, 'L1_ffn_')
    specs.update({k: v + ('in',) for k, v in ffn_l1_dram_specs('f1_').items()})
    staged.update(stage_ffn_l1('f1_', w))
    subs += [ln, (emit_ffn_l1, {'pre': 'f1_'})]

    subs.append((emit_store_resid, {}))
    return subs, specs, staged


def _groupB(inputs):
    specs = base_dram_specs()
    staged = host_const_inputs()
    w = _w_of(inputs, 'L1_attn_b_')
    specs.update({k: v + ('in',)
                  for k, v in attn_l1b_dram_specs('a1b_').items()})
    staged.update(stage_attn_l1b('a1b_', w))
    subs = [(emit_load_resid, {}), (emit_ln_pass, {}),
            (emit_attn_l1b, {'pre': 'a1b_'}), (emit_store_resid, {})]
    return subs, specs, staged


def _groupC(inputs):
    specs = base_dram_specs()
    staged = host_const_inputs()
    subs = [(emit_load_resid, {})]
    ln = (emit_ln_pass, {})

    w = _w_of(inputs, 'L1_ffn_')
    specs.update({k: v + ('in',) for k, v in ffn_l1_dram_specs('f1_').items()})
    staged.update(stage_ffn_l1('f1_', w))
    subs += [ln, (emit_ffn_l1, {'pre': 'f1_'}), ('SNAP', 8)]

    w = _w_of(inputs, 'L2_attn_a_')
    specs.update({k: v + ('in',) for k, v in attn_dram_specs('a2a_').items()})
    staged.update({'a2a_Wq': np.ascontiguousarray(w['qw']),
                   'a2a_Wk': np.ascontiguousarray(w['kw']),
                   'a2a_Wv': np.ascontiguousarray(w['vw']),
                   'a2a_Wo': np.ascontiguousarray(w['ow']),
                   'a2a_smalls': np.zeros((128, NDC, 128), np.float32),
                   'a2a_bvrow': np.zeros((128, D), np.float32)})
    subs += [ln, (emit_attn, {'pre': 'a2a_',
                              'cfg': {'causal': True, 'scale': 0.03125}})]
    subs += [('SNAP', 9)]

    w = _w_of(inputs, 'L2_ffn_')
    specs.update({k: v + ('in',) for k, v in ffn_l2_dram_specs('f2_').items()})
    staged.update(stage_ffn_l2('f2_', w))
    ffn2 = [ln, (emit_ffn_moe, {'pre': 'f2_', 'mode': 'l2', 'NF': 16})]
    subs += ffn2 + [('SNAP', 10)]

    w = _w_of(inputs, 'L2_attn_b_')
    specs.update({k: v + ('in',)
                  for k, v in attn_l2b_dram_specs('a2b_').items()})
    staged.update(stage_attn_l2b('a2b_', w))
    specs['a2b_abcd'] = ([16, 3, T], BF16, 'internal')
    subs += [ln, (emit_attn, {'pre': 'a2b_',
                              'cfg': {'causal': False, 'scale': 0.125,
                                      'gbias': True}})]
    subs += [('SNAP', 11)] + ffn2 + [('SNAP', 12)]

    w = _w_of(inputs, 'L3_attn_a_')
    specs.update({k: v + ('in',)
                  for k, v in attn_l3a_dram_specs('a3a_').items()})
    staged.update(stage_attn_l3a('a3a_', w))
    subs += [ln, (emit_attn_l3a, {'pre': 'a3a_'}), ('SNAP', 13)]

    w = _w_of(inputs, 'L3_ffn_')
    specs.update({k: v + ('in',) for k, v in ffn_l3_dram_specs('f3_').items()})
    staged.update(stage_ffn_l3('f3_', w))
    ffn3 = [ln, (emit_ffn_l3, {'pre': 'f3_'})]
    subs += ffn3 + [('SNAP', 14)]

    w = _w_of(inputs, 'L3_attn_b_')
    specs.update({k: v + ('in',)
                  for k, v in attn_rope_dram_specs('a3b_').items()})
    staged.update(stage_attn_rope('a3b_', w))
    subs += [ln, (emit_attn, {'pre': 'a3b_',
                              'cfg': {'rope': 'l3b', 'causal': True,
                                      'scale': 0.125}})]
    subs += [('SNAP', 15)] + ffn3

    subs.append((emit_store_resid, {}))
    return subs, specs, staged


def kernel(**inputs):
    x = np.ascontiguousarray(inputs['L0_attn_a_x'], dtype=np.float32)

    subs, specs, staged = _groupA(inputs)
    subs = [s for s in subs if s[0] != 'SNAP']
    nc, _ = build_program(subs, specs)
    in_maps = [dict(staged, x_in=np.ascontiguousarray(x[b].T))
               for b in range(8)]
    res = run_bass_kernel_spmd(nc, in_maps, core_ids=list(range(8)))
    x = np.stack([r['x_out'].T for r in res.results], axis=0)

    subs, specs, staged = _groupB(inputs)
    nc, _ = build_program(subs, specs)
    in_maps = []
    for c in range(8):
        ch = x[:, c * 512:(c + 1) * 512, :]
        in_maps.append(dict(staged, x_in=np.ascontiguousarray(
            ch.reshape(8 * 512, 1024).T)))
    res = run_bass_kernel_spmd(nc, in_maps, core_ids=list(range(8)))
    x2 = np.zeros((8, 4096, 1024), np.float32)
    for c in range(8):
        x2[:, c * 512:(c + 1) * 512, :] = \
            res.results[c]['x_out'].T.reshape(8, 512, 1024)

    subs, specs, staged = _groupC(inputs)
    subs = [s for s in subs if s[0] != 'SNAP']
    nc, _ = build_program(subs, specs)
    in_maps = [dict(staged, x_in=np.ascontiguousarray(x2[b].T))
               for b in range(8)]
    for c in range(8):
        in_maps[c].update(stage_ffn_l3_alpha('f3_', inputs, c))
    res = run_bass_kernel_spmd(nc, in_maps, core_ids=list(range(8)))
    return np.stack([r['x_out'].T for r in res.results], axis=0)


def emit_snap(b, idx):
    p = P(b)
    tok = p.step("gpsimd", [],
                 lambda g: g.dma_start(out=b.dram[f"snap{idx}"].ap(),
                                       in_=b.dram["resid"].ap()),
                 inc="qg", amt=16)
    p.step("gpsimd", [tok], lambda g: g.nop())
    p.flush()
    b.barrier()


def add_snaps(subs, specs, idxs):
    out = []
    for s in subs:
        if isinstance(s, tuple) and s[0] == 'SNAP':
            k = s[1]
            specs[f"snap{k}"] = ([D, T], F32, "out")
            out.append((emit_snap, {'idx': k}))
        else:
            out.append(s)
    return out


Bld.SERIAL = True
